# version 22 (final)
# baseline (speedup 1.0000x reference)
import contextlib
import numpy as np
import ml_dtypes
import concourse.bass as bass
import concourse.mybir as mybir
from concourse.bass_utils import run_bass_kernel_spmd
from concourse.bass_core_rust import IndirectOffsetOnAxis

F32 = mybir.dt.float32
BF16 = mybir.dt.bfloat16
U32 = mybir.dt.uint32
ALU = mybir.AluOpType
ACT = mybir.ActivationFunctionType
AX = mybir.AxisListType

D = 1024
L = 2048
T = 16
H = 16
DFF = 2730
DFFP = 2816
NFFC = 22
NB = 2
EPS_LN = 1e-5
EPS_RMS = 1e-6
EPS_DEN = 1e-6

ENGINES = ("sync", "tensor", "vector", "scalar", "gpsimd")
DEBUG = False
PHASES = "ARF"
STOP_STAGE = 99


class Plan:
    def __init__(self):
        self.recs = {e: [] for e in ENGINES}
        self.cnt = {}

    def op(self, eng, fn, waits=(), inc=None, inc_amt=1, drain=False):
        tok = None
        if inc is not None:
            self.cnt[inc] = self.cnt.get(inc, 0) + inc_amt
            tok = (inc, self.cnt[inc])
        self.recs[eng].append((fn, tuple(w for w in waits if w is not None),
                               inc, inc_amt, drain))
        return tok

    def dma(self, eng, fn, waits=(), inc=None):
        return self.op(eng, fn, waits, inc, inc_amt=16)


def build_program():
    nc = bass.Bass("TRN2", target_bir_lowering=False)

    x_in = nc.dram_tensor("x_in", [NB, L, D], F32, kind="ExternalInput")
    aw_qkv = nc.dram_tensor("aw_qkv", [3 * D, D], F32, kind="ExternalInput")
    aw_out = nc.dram_tensor("aw_out", [D, D], F32, kind="ExternalInput")
    fw_qkv = nc.dram_tensor("fw_qkv", [3 * D, D], F32, kind="ExternalInput")
    fw_out = nc.dram_tensor("fw_out", [D, D], F32, kind="ExternalInput")
    w1 = nc.dram_tensor("w1", [DFF, D], F32, kind="ExternalInput")
    w2 = nc.dram_tensor("w2", [DFF, D], F32, kind="ExternalInput")
    w3 = nc.dram_tensor("w3", [D, DFF], F32, kind="ExternalInput")
    wr_in = nc.dram_tensor("wr_in", [128, D], F32, kind="ExternalInput")
    g1_in = nc.dram_tensor("g1_in", [128, D], BF16, kind="ExternalInput")
    g2_in = nc.dram_tensor("g2_in", [128, D], BF16, kind="ExternalInput")
    c_identf = nc.dram_tensor("c_identf", [128, 128], F32, kind="ExternalInput")
    c_identb = nc.dram_tensor("c_identb", [128, 128], BF16, kind="ExternalInput")
    c_triu = nc.dram_tensor("c_triu", [128, 128], BF16, kind="ExternalInput")
    c_tril = nc.dram_tensor("c_tril", [128, 128], BF16, kind="ExternalInput")
    c_iota512 = nc.dram_tensor("c_iota512", [128, 512], F32, kind="ExternalInput")
    c_idvec = nc.dram_tensor("c_idvec", [128, 16], F32, kind="ExternalInput")
    c_onesb = nc.dram_tensor("c_onesb", [128, 1], BF16, kind="ExternalInput")

    out = nc.dram_tensor("out", [NB, L, D], F32, kind="ExternalOutput")
    if DEBUG:
        dbg_x1 = nc.dram_tensor("dbg_x1", [NB, L, D], F32, kind="ExternalOutput")
        dbg_rank = nc.dram_tensor("dbg_rank", [NB, 2048], F32,
                                  kind="ExternalOutput")
        dbg_sel = nc.dram_tensor("dbg_sel", [NB, 128, 4], U32,
                                 kind="ExternalOutput")

    aw_qkv_b = nc.dram_tensor("aw_qkv_b", [3 * D, D], BF16)
    aw_out_b = nc.dram_tensor("aw_out_b", [D, D], BF16)
    fw_qkv_b = nc.dram_tensor("fw_qkv_b", [3 * D, D], BF16)
    fw_out_b = nc.dram_tensor("fw_out_b", [D, D], BF16)
    w1_b = nc.dram_tensor("w1_b", [DFFP, D], BF16)
    w2_b = nc.dram_tensor("w2_b", [DFFP, D], BF16)
    w3_b = nc.dram_tensor("w3_b", [D, DFFP], BF16)
    x1scr = nc.dram_tensor("x1scr", [NB, L, D], BF16)
    srow_scr = nc.dram_tensor("srow_scr", [NB, 2048], F32)
    ln2scr = nc.dram_tensor("ln2scr", [NB, L, D], BF16)

    p = Plan()

    with contextlib.ExitStack() as ctx:
        def sbt(name, shape, dt):
            return ctx.enter_context(nc.sbuf_tensor(name, shape, dt))

        arena1 = sbt("arena1", [128, 32768], BF16)
        arena2 = sbt("arena2", [128, 30720], BF16)
        ubuf = sbt("ubuf", [128, 18432], BF16)
        x_sb = sbt("x_sb", [128, 3072], F32)
        fscr = sbt("fscr", [128, 2048], F32)
        fscr2 = sbt("fscr2", [128, 1536], BF16)
        ptstg = sbt("ptstg", [128, 1024], F32)
        selstg = sbt("selstg", [128, 1024], F32)
        wrbc = sbt("wrbc", [128, 1024], F32)
        g1bc = sbt("g1bc", [128, 1024], BF16)
        g2bc = sbt("g2bc", [128, 1024], BF16)
        identf = sbt("identf", [128, 128], F32)
        identb = sbt("identb", [128, 128], BF16)
        triu = sbt("triu", [128, 128], BF16)
        tril = sbt("tril", [128, 128], BF16)
        iota512 = sbt("iota512", [128, 512], F32)
        idvec = sbt("idvec", [128, 16], F32)
        onesb = sbt("onesb", [128, 1], BF16)
        S_sb = sbt("S_sb", [64, 1040], BF16)
        S_sb_holder = [S_sb]
        s_pm = sbt("s_pm", [128, 32], F32)
        selid = sbt("selid", [128, 8], U32)
        sumw = sbt("sumw", [128, 1], F32)
        sm = sbt("sm", [128, 56], F32)
        rank_pm = sbt("rank_pm", [128, 16], F32)
        zero16 = sbt("zero16", [128, 1024], BF16)

        UW = ubuf.shape[1]
        A1W = arena1.shape[1]
        SB0, SBP = 0, 33

        U_xln, U_xlnT, U_qk, U_v1, U_qkT, U_at, U_o, U_oT, U_stg = \
            0, 1024, 2048, 4096, 5184, 9280, 11328, 12352, 13376
        R_gt, R_eq, R_diag = 0, 4096, 8192
        F_xsel = 0
        F_h1T = 4096
        F_qkTs = 4096
        F_qk = 8192
        F_AT1 = 8192
        F_pt = 16384
        F_o, F_oT = 16384, 17408
        F_stg2 = 16384
        NFF_V1 = 22528

        sems = {n: ctx.enter_context(nc.semaphore(n)) for n in
                ["ld_c", "ld_x", "ld_xa", "ld_x1", "ld_x2", "ld_g", "t_awo", "st_scr", "st_out",
                 "s_pe", "s_dv", "s_sc",
                 "c_aw1", "c_aw2", "c_fw1", "c_fw2", "c_w1", "c_w2", "c_w3",
                 "c_pad", "st_sc2", "st_xc", "st_dbg", "ld_p0", "ld_p1", "ld_g1", "ld_g2", "ld_g3",
                 "t_aw", "t_fwq", "t_wfo", "t_w2", "t_w1", "t_w3", "r_bs"]}

        def tpose(e, dst, src, ident):
            return e.matmul(dst, src, ident, start=True, stop=True,
                            is_transpose=True)

        tok_c = None
        for dst, src in ((wrbc[:], wr_in[:, :]), (g1bc[:], g1_in[:, :]),
                         (g2bc[:], g2_in[:, :]), (identf[:], c_identf[:, :]),
                         (identb[:], c_identb[:, :]), (triu[:], c_triu[:, :]),
                         (tril[:], c_tril[:, :]), (iota512[:], c_iota512[:, :]),
                         (idvec[:], c_idvec[:, :]),
                         (onesb[:], c_onesb[:, :])):
            tok_c = p.dma("sync", lambda e, d=dst, s=src:
                          e.dma_start(out=d, in_=s), inc="ld_c")

        def cast_weight(dst, src, rows, cols, sem, rl=512):
            tok = None
            r = 0
            while r < rows:
                r2 = min(r + rl, rows)
                tok = p.dma("gpsimd", lambda e, d=dst, s=src, a=r, b2=r2:
                            e.dma_start(out=d[a:b2, 0:cols],
                                        in_=s[a:b2, 0:cols]),
                            inc=sem)
                r = r2
            return tok

        tok_cast_aw1 = cast_weight(aw_qkv_b, aw_qkv, 3 * D, D, "c_aw1")
        tok_cast_aw2 = cast_weight(aw_out_b, aw_out, D, D, "c_aw2")
        p.op("vector", lambda e: e.memset(zero16[:], 0.0))
        tok_z = p.op("vector", lambda e: None, (), "s_dv", 1, drain=True)
        p.dma("gpsimd", lambda e: e.dma_start(out=w1_b[DFF:DFFP, :],
                                              in_=zero16[0:DFFP - DFF, :]),
              waits=(tok_z,), inc="c_pad")
        p.dma("gpsimd", lambda e: e.dma_start(out=w2_b[DFF:DFFP, :],
                                              in_=zero16[0:DFFP - DFF, :]),
              waits=(tok_z,), inc="c_pad")
        tok_pad = None
        for i in range(8):
            tok_pad = p.dma("gpsimd", lambda e, i=i: e.dma_start(
                out=w3_b[i * 128:(i + 1) * 128, DFF:DFFP],
                in_=zero16[0:128, 0:DFFP - DFF]), waits=(tok_z,), inc="c_pad")
        deferred_casts = []

        def defer_cast(dst, src, rows, cols, sem, rl=512):
            toks = {}
            r = 0
            while r < rows:
                r2 = min(r + rl, rows)
                def emit1(a=r, b2=r2, dst=dst, src=src, cols=cols, sem=sem,
                          toks=toks):
                    toks["t"] = p.dma(
                        "gpsimd", lambda e, d=dst, s=src, a=a, b2=b2:
                        e.dma_start(out=d[a:b2, 0:cols],
                                    in_=s[a:b2, 0:cols]), inc=sem)
                deferred_casts.append(emit1)
                r = r2
            return toks

        tokd_fw1 = defer_cast(fw_qkv_b, fw_qkv, 3 * D, D, "c_fw1")
        tokd_fw2 = defer_cast(fw_out_b, fw_out, D, D, "c_fw2")
        tokd_w1 = defer_cast(w1_b, w1, DFF, D, "c_w1")
        tokd_w2 = defer_cast(w2_b, w2, DFF, D, "c_w2")
        tokd_w3 = defer_cast(w3_b, w3, D, DFF, "c_w3")
        n_fw1_chunks = 6

        def tload(arena, col, src_b, kchunk, width, wait, sem):
            if wait is None:
                wl = ()
            elif isinstance(wait, tuple) and len(wait) == 2 \
                    and isinstance(wait[0], str):
                wl = (wait,)
            else:
                wl = tuple(wait)
            return p.dma("sync", lambda e, a=arena, c=col, s=src_b, k=kchunk,
                         w=width: e.dma_start_transpose(
                             a[:, c:c + w], s[:, k * 128:(k + 1) * 128]),
                         waits=wl, inc=sem)

        for k in range(8):
            tok_wa = tload(arena1, k * 3072, aw_qkv_b, k, 3072, tok_cast_aw1,
                           "t_aw")
        tok_awo = None
        for k in range(8):
            tok_awo = tload(arena1, 24576 + k * 1024, aw_out_b, k, 1024,
                            tok_cast_aw2, "t_awo")

        p.op("vector", lambda e: e.reduce_sum(sumw[:], wrbc[:], AX.X),
             waits=(tok_c,))
        p.op("vector", lambda e: e.memset(
            bass.AP(ubuf, U_v1 + 64, [[UW, 128], [65, 16], [1, 1]]), 1.0))
        p.op("vector", lambda e: e.memset(S_sb_holder[0][:], 0.0))
        tok_ones = p.op("vector", lambda e: None, (), "s_dv", 1, drain=True)

        def stats(c0, n, eps, toks):
            def f1(e):
                e.tensor_scalar(sm[:, c0 + 2:c0 + 3], sm[:, c0:c0 + 1],
                                1.0 / n, None, ALU.mult)
                e.drain()
                e.tensor_tensor(sm[:, c0 + 5:c0 + 6], sm[:, c0 + 2:c0 + 3],
                                sm[:, c0 + 2:c0 + 3], ALU.mult)
                e.drain()
                e.scalar_tensor_tensor(sm[:, c0 + 6:c0 + 7],
                                       sm[:, c0 + 1:c0 + 2], 1.0 / n,
                                       sm[:, c0 + 5:c0 + 6], ALU.mult,
                                       ALU.subtract)
                e.drain()
                return e.tensor_scalar(sm[:, c0 + 6:c0 + 7],
                                       sm[:, c0 + 6:c0 + 7], float(eps),
                                       None, ALU.add)
            t1 = p.op("vector", f1, waits=toks[1:], inc="s_dv", drain=True)

            def f2s(e):
                if c0 == 0:
                    e.activation(sm[:, c0 + 7:c0 + 8], sm[:, c0 + 6:c0 + 7],
                                 ACT.Ln)
                elif c0 == 8:
                    e.activation(sm[:, c0 + 7:c0 + 8], sm[:, c0 + 6:c0 + 7],
                                 ACT.Ln)
                elif c0 == 41:
                    e.activation(sm[:, c0 + 7:c0 + 8], sm[:, c0 + 6:c0 + 7],
                                 ACT.Ln)
                else:
                    e.activation(sm[:, c0 + 7:c0 + 8], sm[:, c0 + 6:c0 + 7],
                                 ACT.Ln)
                e.drain()
                return e.activation(sm[:, c0 + 3:c0 + 4],
                                    sm[:, c0 + 7:c0 + 8], ACT.Exp,
                                    scale=-0.5)
            t2 = p.op("scalar", f2s, waits=(t1,), inc="s_sc", drain=True)
            return p.op("vector", lambda e: e.scalar_tensor_tensor(
                sm[:, c0 + 4:c0 + 5], sm[:, c0 + 2:c0 + 3], -1.0,
                sm[:, c0 + 3:c0 + 4], ALU.mult, ALU.mult),
                waits=(t2,), inc="s_dv", drain=True)

        def rstats(c0, n, eps, toks):
            t1 = p.op("vector", lambda e: e.tensor_scalar(
                sm[:, c0 + 2:c0 + 3], sm[:, c0:c0 + 1], 1.0 / n, float(eps),
                ALU.mult, ALU.add), waits=toks, inc="s_dv", drain=True)

            def f2s(e):
                e.activation(sm[:, c0 + 3:c0 + 4], sm[:, c0 + 2:c0 + 3],
                             ACT.Ln)
                e.drain()
                return e.activation(sm[:, c0 + 1:c0 + 2],
                                    sm[:, c0 + 3:c0 + 4], ACT.Exp,
                                    scale=-0.5)
            return p.op("scalar", f2s, waits=(t1,), inc="s_sc", drain=True)

        HBANKS = ((0, 7), (7, 7), (14, 2))

        def oc(h):
            return h // 7, 65 * (h % 7)

        def den_norm(pb, out_tensor, out_col, wait_tok, eps):
            def f_den(e):
                for m, (h0, nh) in enumerate(HBANKS):
                    src = pb[m][:, 0:65 * nh].rearrange(
                        "p (j e) -> p j e", e=65)[:, :, 64:65]
                    e.tensor_scalar_max(sm[:, 16 + h0:16 + h0 + nh], src,
                                        float(eps))
                    e.drain()
                return e.reciprocal(sm[:, 16:32], sm[:, 16:32])
            t_dn = p.op("vector", f_den, waits=(wait_tok,), inc="s_dv",
                        drain=True)

            on_toks = []
            for m, (h0, nh) in enumerate(HBANKS):
                def f_on(e, m=m, h0=h0, nh=nh):
                    src = pb[m][:, 0:65 * nh].rearrange(
                        "p (j e) -> p j e", e=65)[:, :, 0:64]
                    dst = out_tensor[:, out_col + 64 * h0:
                                     out_col + 64 * (h0 + nh)].rearrange(
                        "p (j e) -> p j e", e=64)
                    rec = bass.AP(sm, 16 + h0,
                                  [[sm.shape[1], 128], [1, nh],
                                   [0, 64]])
                    return e.tensor_tensor(dst, src, rec, ALU.mult)
                on_toks.append(p.op("vector", f_on, waits=(t_dn,),
                                    inc="s_dv", drain=True))
            return on_toks

        pa_ctx = contextlib.ExitStack()
        p_tpb = pa_ctx.enter_context(nc.psum_tensor("p_tpb", [128, 2048], BF16))
        pf = [pa_ctx.enter_context(nc.psum_tensor(f"pf{i}", [128, 512], F32))
              for i in range(3)]
        p_S = pa_ctx.enter_context(nc.psum_tensor("p_S", [64, 1536], F32))

        tiles = [(b, t) for b in range(NB) for t in range(T)]
        NTI = len(tiles)
        TK = [dict() for _ in range(NTI)]

        def emit_xloadA(i):
            b, t = tiles[i]
            d = TK[i]
            xb = i % 3
            xfree = TK[i - 3]["x1"] if i >= 3 else None
            d["x"] = p.dma("scalar", lambda e, b=b, t=t, xb=xb: e.dma_start(
                out=x_sb[:, xb * 1024:xb * 1024 + 1024],
                in_=x_in[b, t * 128:(t + 1) * 128, :]),
                waits=(xfree,), inc=("ld_xa", "ld_x1", "ld_x2")[xb])

        def emit_headA(i):
            d = TK[i]
            xb = i % 3
            d["s"] = p.op("vector", lambda e, xb=xb: e.reduce_sum(
                sm[:, 0:1], x_sb[:, xb * 1024:xb * 1024 + 1024], AX.X),
                waits=(d["x"],), inc="s_dv", drain=True)
            d["q"] = p.op("scalar", lambda e, xb=xb: e.activation(
                ubuf[:, U_stg:U_stg + 1024],
                x_sb[:, xb * 1024:xb * 1024 + 1024], ACT.Square,
                accum_out=sm[:, 1:2]), waits=(d["x"],), inc="s_sc",
                drain=True)
            t_st = stats(SB0, D, EPS_LN, (d["s"], d["q"]))
            d["xln"] = p.op("scalar", lambda e, xb=xb: e.activation(
                ubuf[:, U_xln:U_xln + 1024],
                x_sb[:, xb * 1024:xb * 1024 + 1024], ACT.Identity,
                bias=sm[:, 4:5], scale=sm[:, 3:4]),
                waits=(t_st,), inc="s_sc", drain=True)

        def emit_midA_pre(i):
            d = TK[i]
            def f_tp(e):
                r = None
                for j in range(8):
                    r = tpose(e, p_tpb[:, j * 128:(j + 1) * 128],
                              ubuf[:, U_xln + j * 128:U_xln + (j + 1) * 128],
                              identb[:])
                return r
            d["tp"] = p.op("tensor", f_tp,
                           waits=(d["xln"],
                                  TK[i - 1]["oT"] if i else None)
                           + ((tok_c,) if i == 0 else ()), inc="s_pe")
            d["xlnT"] = p.op("scalar", lambda e: e.copy(
                ubuf[:, U_xlnT:U_xlnT + 1024], p_tpb[:, 0:1024]),
                waits=(d["tp"],), inc="s_sc", drain=True)
            d["phi"] = []
            d["v"] = []

        def emit_bankA(i, bk):
            d = TK[i]
            nb_ = bk % 3
            n0 = bk * 512

            def f_mmb(e, nb_=nb_, n0=n0):
                r = None
                for k in range(8):
                    r = e.matmul(
                        pf[nb_][:, 0:512],
                        ubuf[:, U_xlnT + k * 128:U_xlnT + (k + 1) * 128],
                        arena1[:, k * 3072 + n0:k * 3072 + n0 + 512],
                        start=(k == 0), stop=(k == 7))
                return r
            waits = [d["xlnT"]]
            if bk >= 3:
                waits.append(d["phi"][bk - 3])
            if bk < 2 and i > 0:
                waits.append(TK[i - 1]["aoev"])
            if i == 0 and bk == 0:
                waits.append(tok_wa)
            t_mmb = p.op("tensor", f_mmb, waits=tuple(waits), inc="s_pe")
            d["mm%d" % bk] = t_mmb
            if bk < 4:
                t_minb = p.op("vector", lambda e, nb_=nb_:
                              e.tensor_scalar_min(
                                  fscr2[:, nb_ * 512:nb_ * 512 + 512],
                                  pf[nb_][:, 0:512], 0.0),
                              waits=(t_mmb,), inc="s_dv", drain=True)
                t_expb = p.op("scalar", lambda e, nb_=nb_: e.activation(
                    fscr2[:, nb_ * 512:nb_ * 512 + 512],
                    fscr2[:, nb_ * 512:nb_ * 512 + 512], ACT.Exp),
                    waits=(t_minb,), inc="s_sc", drain=True)
                t_phib = p.op("vector", lambda e, bk=bk, nb_=nb_:
                              e.scalar_tensor_tensor(
                                  ubuf[:, U_qk + bk * 512:
                                       U_qk + bk * 512 + 512],
                                  pf[nb_][:, 0:512], 0.0,
                                  fscr2[:, nb_ * 512:nb_ * 512 + 512],
                                  ALU.max, ALU.add),
                              waits=(t_expb,), inc="s_dv", drain=True)
                d["phi"].append(t_phib)
            else:
                t_vb = p.op("vector", lambda e, bk=bk, nb_=nb_:
                            e.tensor_copy(
                                bass.AP(ubuf, U_v1 + 65 * 8 * (bk - 4),
                                        [[UW, 128], [65, 8], [1, 64]]),
                                pf[nb_][:, 0:512]), waits=(t_mmb,),
                            inc="s_dv", drain=True)
                d["v"].append(t_vb)

        def emit_tqA(i):
            d = TK[i]
            RND = ((0, 0, 0), (16, 2048, 1024), (8, 1024, 0), (24, 3072, 1024))
            PHI = (0, 2, 1, 3)
            evs = []
            for r, (s0, dcol, pbase) in enumerate(RND):
                def f_tq(e, s0=s0, pbase=pbase):
                    r_ = None
                    for j in range(8):
                        r_ = tpose(e,
                                   p_tpb[0:64, pbase + j * 128:
                                         pbase + (j + 1) * 128],
                                   ubuf[:, U_qk + (s0 + j) * 64:
                                        U_qk + (s0 + j) * 64 + 64],
                                   identb[:])
                    return r_
                t_tq = p.op("tensor", f_tq,
                            waits=(d["phi"][PHI[r]],
                                   evs[r - 2] if r >= 2 else None),
                            inc="s_pe")
                ev = p.op("scalar", lambda e, dcol=dcol, pbase=pbase: e.copy(
                    ubuf[0:64, U_qkT + dcol:U_qkT + dcol + 1024],
                    p_tpb[0:64, pbase:pbase + 1024]),
                    waits=(t_tq,), inc="s_sc", drain=True)
                evs.append(ev)
            d["qkT"] = evs

        def emit_midA_post(i):
            b, t = tiles[i]
            d = TK[i]
            t_atm = None
            am_toks = []
            for rnd in range(2):
                def f_at(e, rnd=rnd):
                    r = None
                    for j in range(8):
                        jj = (j // 2) + 4 * (j % 2)
                        h = rnd * 8 + jj
                        r = e.matmul(pf[jj // 4][:, (jj % 4) * 128:
                                                 (jj % 4) * 128 + 128],
                                     kT_ap(h), qT_ap(h),
                                     start=True, stop=True)
                    return r
                at_w = (d["qkT"][0], d["qkT"][1], d["phi"][3], d["v"][0]) \
                    if rnd == 0 else (d["qkT"][2], d["qkT"][3], t_atm)
                t_at = p.op("tensor", f_at, waits=at_w, inc="s_pe")

                def f_am(e, rnd=rnd):
                    r = None
                    for j in range(2):
                        r = e.tensor_tensor(
                            ubuf[:, U_at + rnd * 1024 + j * 512:
                                 U_at + rnd * 1024 + j * 512 + 512],
                            pf[j][:, 0:512],
                            bass.AP(triu, 0,
                                    [[128, 128], [0, 4], [1, 128]]),
                            ALU.mult)
                    return r
                t_atm = p.op("vector", f_am, waits=(t_at,), inc="s_dv",
                             drain=True)
                am_toks.append(t_atm)

            def f_o(e, t=t):
                r = None
                for h in range(H):
                    bank, c0 = oc(h)
                    if t > 0:
                        r = e.matmul(
                            pf[bank][:, c0:c0 + 65], qT_ap(h),
                            S_sb[:, 65 * h:65 * h + 65],
                            start=(h in (0, 7, 14)), stop=False)
                    r = e.matmul(
                        pf[bank][:, c0:c0 + 65],
                        ubuf[:, U_at + h * 128:U_at + (h + 1) * 128],
                        ubuf[:, U_v1 + 65 * h:U_v1 + 65 * h + 65],
                        start=(t == 0 and h in (0, 7, 14)),
                        stop=(h in (6, 13, 15)))
                return r
            t_o = p.op("tensor", f_o,
                       waits=(t_atm, d["v"][1],
                              TK[i - 1]["scp"] if i else None)
                       + ((tok_ones,) if i == 0 else ()),
                       inc="s_pe")

            def f_kv(e, t=t):
                r = None
                for h in range(H):
                    bank, c0 = oc(h)
                    r = e.matmul(
                        p_S[:, 512 * bank + c0:512 * bank + c0 + 65],
                        ubuf[:, U_qk + 1024 + h * 64:
                             U_qk + 1024 + (h + 1) * 64],
                        ubuf[:, U_v1 + 65 * h:U_v1 + 65 * h + 65],
                        start=(t == 0 and h in (0, 7, 14)),
                        stop=(h in (6, 13, 15)),
                        skip_group_check=(t > 0))
                return r
            t_kv = p.op("tensor", f_kv,
                        waits=(t_atm, d["v"][1])
                        + ((tok_ones,) if i == 0 else ()), inc="s_pe")

            on_toks = den_norm(pf, ubuf, U_o, t_o, EPS_DEN)
            t_ob = on_toks[2]
            t_to = None
            for j0, j1, w in ((0, 3, (on_toks[0], d["qkT"][2])),
                              (3, 7, (on_toks[1],)),
                              (7, 8, (on_toks[2],))):
                t_to = p.op("tensor", lambda e, j0=j0, j1=j1: [
                    tpose(e, p_tpb[:, j * 128:(j + 1) * 128],
                          ubuf[:, U_o + j * 128:U_o + (j + 1) * 128],
                          identb[:]) for j in range(j0, j1)][-1],
                    waits=w + (t_to,), inc="s_pe")

            def f_scp(e):
                r = None
                for m, (h0, nh) in enumerate(HBANKS):
                    r = e.copy(S_sb[:, 65 * h0:65 * (h0 + nh)],
                               p_S[:, 512 * m:512 * m + 65 * nh])
                    e.drain()
                return r
            d["scp"] = p.op("scalar", f_scp, waits=(t_kv,), inc="s_sc",
                            drain=True)
            d["oT"] = p.op("scalar", lambda e: e.copy(
                ubuf[:, U_oT:U_oT + 1024], p_tpb[:, 0:1024]),
                waits=(t_to,), inc="s_sc", drain=True)

        def emit_woA(i):
            d = TK[i]

            def f_wo(e):
                r = None
                for nb_ in range(2):
                    for k in range(8):
                        r = e.matmul(
                            pf[nb_][:, 0:512],
                            ubuf[:, U_oT + k * 128:U_oT + (k + 1) * 128],
                            arena1[:, 24576 + k * 1024 + nb_ * 512:
                                   24576 + k * 1024 + nb_ * 512 + 512],
                            start=(k == 0), stop=(k == 7))
                return r
            d["wo"] = p.op("tensor", f_wo,
                           waits=(d["oT"],)
                           + ((tok_awo,) if i == 0 else ()), inc="s_pe")

        def emit_aoevA(j):
            d = TK[j]

            def f_aoev(e):
                e.copy(fscr[:, 0:512], pf[0][:, 0:512])
                e.drain()
                return e.copy(fscr[:, 512:1024], pf[1][:, 0:512])
            d["aoev"] = p.op("scalar", f_aoev,
                             waits=(d["wo"],
                                    TK[j - 1]["x1"] if j else None),
                             inc="s_sc", drain=True)

        def emit_tailA1(j):
            d = TK[j]
            d["s2"] = p.op("vector", lambda e: e.reduce_sum(
                sm[:, 8:9], fscr[:, 0:1024], AX.X), waits=(d["aoev"],),
                inc="s_dv", drain=True)
            d["q2"] = p.op("scalar", lambda e: e.activation(
                ubuf[:, U_stg:U_stg + 1024], fscr[:, 0:1024], ACT.Square,
                accum_out=sm[:, 9:10]), waits=(d["aoev"],), inc="s_sc",
                drain=True)

        def emit_tailA2(j):
            d = TK[j]
            d["st2"] = stats(8, D, EPS_LN, (d["s2"], d["q2"]))

        def emit_tailA3(j):
            d = TK[j]

            def f_x1(e, j=j):
                e.tensor_scalar(fscr[:, 0:1024], fscr[:, 0:1024],
                                sm[:, 11:12], sm[:, 12:13],
                                ALU.mult, ALU.add)
                e.drain()
                xb = j % 3
                return e.tensor_tensor(fscr[:, 1024:2048], fscr[:, 0:1024],
                                       x_sb[:, xb * 1024:xb * 1024 + 1024],
                                       ALU.add)
            d["x1"] = p.op("vector", f_x1,
                           waits=(TK[j - 1]["stg"] if j else None,
                                  TK[j - 1]["xcast"] if j else None),
                           inc="s_dv", drain=True)
            b_, t_ = tiles[j]
            d["xcast"] = p.dma("gpsimd", lambda e, b=b_, t=t_: e.dma_start(
                out=x1scr[b, t * 128:(t + 1) * 128, :],
                in_=fscr[:, 1024:2048]), waits=(d["x1"],), inc="st_xc")
            t_s3 = p.op("vector", lambda e: e.reduce_sum(
                sm[:, 41:42], fscr[:, 1024:2048], AX.X), waits=(),
                inc="s_dv", drain=True)
            d["q3"] = p.op("scalar", lambda e: e.activation(
                ubuf[:, U_stg:U_stg + 1024], fscr[:, 1024:2048], ACT.Square,
                accum_out=sm[:, 42:43]), waits=(d["x1"],), inc="s_sc",
                drain=True)
            d["st3"] = stats(41, D, EPS_LN, (t_s3, d["q3"]))

        def emit_tailA4(j):
            b, t = tiles[j]
            d = TK[j]

            def f_sc(e, b=b, t=t):
                e.tensor_tensor(selstg[:], fscr[:, 1024:2048],
                                wrbc[:], ALU.mult)
                e.drain()
                e.reduce_sum(sm[:, 49:50], selstg[:], AX.X)
                e.drain()
                e.tensor_tensor(sm[:, 50:51], sm[:, 43:44], sumw[:],
                                ALU.mult)
                e.drain()
                e.tensor_tensor(sm[:, 49:50], sm[:, 49:50], sm[:, 50:51],
                                ALU.subtract)
                e.drain()
                return e.tensor_tensor(s_pm[:, b * 16 + t:b * 16 + t + 1],
                                       sm[:, 49:50], sm[:, 44:45],
                                       ALU.mult)
            d["sc"] = p.op("vector", f_sc, waits=(d["q3"],),
                           inc="s_dv", drain=True)

            def f_stg(e):
                return e.activation(ubuf[:, U_stg + 1024:U_stg + 2048],
                                    fscr[:, 1024:2048], ACT.Identity,
                                    bias=sm[:, 45:46], scale=sm[:, 44:45])
            d["stg"] = p.op("scalar", f_stg,
                            waits=(d["st3"],
                                   TK[j - 1]["stgfree"] if j else None),
                            inc="s_sc", drain=True)
            d["stgfree"] = p.dma("sync", lambda e, b=b, t=t: e.dma_start(
                out=ln2scr[b, t * 128:(t + 1) * 128, :],
                in_=ubuf[:, U_stg + 1024:U_stg + 2048]), waits=(d["stg"],),
                inc="st_scr")
            if DEBUG:
                d["xcast"] = p.dma("sync", lambda e, b=b, t=t:
                                   e.dma_start(
                    out=dbg_x1[b, t * 128:(t + 1) * 128, :],
                    in_=fscr[:, 1024:2048]), waits=(d["sc"], d["stg"],
                                                    d["xcast"]),
                    inc="st_dbg")

        pt_a_toks = {0: [], 1: []}
        pt_chain = [None]
        PTS = [(0, t) for t in range(T)] + [(1, t) for t in range(13)]
        PTK = [dict() for _ in PTS]

        def emit_ptile(b, t, ycol, xcol, extra_waits=()):
            emit_pt_ld(b, t, ycol, xcol, None, extra_waits)
            emit_pt_sc(-1)
            emit_pt_v(-1)

        _lastpt = [None]

        def emit_pt_ld(b, t, ycol, xcol, k, extra_waits=()):
            d = dict(b=b, t=t, ycol=ycol, xcol=xcol)
            d["ly"] = p.dma("scalar", lambda e, b=b, t=t, ycol=ycol:
                            e.dma_start(
                out=ubuf[:, ycol:ycol + 1024],
                in_=ln2scr[b, t * 128:(t + 1) * 128, :]),
                waits=(pt_chain[0],) + tuple(extra_waits), inc="ld_p0")
            d["lx"] = p.dma("scalar", lambda e, b=b, t=t, xcol=xcol:
                            e.dma_start(
                out=ubuf[:, xcol:xcol + 1024],
                in_=x1scr[b, t * 128:(t + 1) * 128, :]),
                waits=(pt_chain[0],) + tuple(extra_waits), inc="ld_p1")
            if k is not None:
                PTK[k].update(d)
            _lastpt[0] = d

        def emit_pt_sc(k):
            d = PTK[k] if k >= 0 else _lastpt[0]
            ycol = d["ycol"]
            t_q = p.op("scalar", lambda e, ycol=ycol: e.activation(
                ptstg[:], ubuf[:, ycol:ycol + 1024], ACT.Square,
                accum_out=sm[:, SBP + 1:SBP + 2]), waits=(d["ly"],),
                inc="s_sc", drain=True)
            t_f1 = p.op("vector", lambda e: e.tensor_scalar(
                sm[:, SBP + 2:SBP + 3], sm[:, SBP + 1:SBP + 2],
                1.0 / D, float(EPS_LN), ALU.mult, ALU.add),
                waits=(t_q,), inc="s_dv", drain=True)

            def f2s(e):
                e.activation(sm[:, SBP + 4:SBP + 5], sm[:, SBP + 2:SBP + 3],
                             ACT.Ln)
                e.drain()
                return e.activation(sm[:, SBP + 3:SBP + 4],
                                    sm[:, SBP + 4:SBP + 5], ACT.Exp,
                                    scale=-0.5)
            d["ps"] = p.op("scalar", f2s, waits=(t_f1,), inc="s_sc",
                           drain=True)

        def emit_pt_v(k):
            d = PTK[k] if k >= 0 else _lastpt[0]
            ycol, xcol, b, t = d["ycol"], d["xcol"], d["b"], d["t"]

            def f_out(e, ycol=ycol, xcol=xcol):
                e.tensor_scalar(ptstg[:], ubuf[:, ycol:ycol + 1024],
                                sm[:, SBP + 3:SBP + 4], None, ALU.mult)
                e.drain()
                return e.tensor_tensor(ptstg[:], ptstg[:],
                                       ubuf[:, xcol:xcol + 1024], ALU.add)
            t_out = p.op("vector", f_out, waits=(d["ps"], d["lx"]),
                         inc="s_dv", drain=True)
            tok_st = p.dma("sync", lambda e, b=b, t=t: e.dma_start(
                out=out[b, t * 128:(t + 1) * 128, :], in_=ptstg[:]),
                waits=(t_out,), inc="st_out")
            pt_chain[0] = tok_st
            pt_a_toks[b].append(tok_st)

        def qT_ap(h):
            return ubuf[0:64, U_qkT + h * 128:U_qkT + h * 128 + 128]

        def kT_ap(h):
            return ubuf[0:64, U_qkT + 2048 + h * 128:
                        U_qkT + 2048 + h * 128 + 128]

        emit_xloadA(0)
        emit_xloadA(1)
        emit_headA(0)
        emit_midA_pre(0)
        for i in range(NTI):
            emit_bankA(i, 0)
            if 0 <= i - 3 < len(PTS):
                emit_pt_sc(i - 3)
            if i >= 1:
                emit_tailA1(i - 1)
            for bk in range(1, 4):
                emit_bankA(i, bk)
            if i >= 1:
                emit_tailA2(i - 1)
            emit_bankA(i, 4)
            emit_bankA(i, 5)
            if 0 <= i - 3 < len(PTS):
                emit_pt_v(i - 3)
            emit_tqA(i)
            if i + 1 < NTI:
                emit_headA(i + 1)
            emit_midA_post(i)
            if i >= 1:
                emit_tailA3(i - 1)
                if deferred_casts:
                    deferred_casts.pop(0)()
                    n_fw1_chunks -= 1
                    if n_fw1_chunks == 0:
                        for k in range(8):
                            tok_fwq = tload(arena2, k * 3072, fw_qkv_b, k,
                                            3072, tokd_fw1["t"], "t_fwq")
            if i + 1 < NTI:
                emit_midA_pre(i + 1)
            emit_woA(i)
            emit_aoevA(i)
            if i + 2 < NTI:
                emit_xloadA(i + 2)
            if 0 <= i - 2 < len(PTS):
                bpt, tpt = PTS[i - 2]
                src = TK[i - 2]
                emit_pt_ld(bpt, tpt, 15424, 16448, i - 2,
                           (src["stgfree"], src["xcast"]))
            if i >= 1:
                emit_tailA4(i - 1)
        emit_tailA1(NTI - 1)
        emit_tailA2(NTI - 1)
        emit_tailA3(NTI - 1)
        emit_tailA4(NTI - 1)

        last_scr_tok = [(TK[T - 1]["stgfree"], TK[T - 1]["xcast"]),
                        (TK[NTI - 1]["stgfree"], TK[NTI - 1]["xcast"])]
        phaseA_finals = [TK[NTI - 1]["stgfree"], TK[NTI - 1]["scp"],
                         TK[NTI - 1]["sc"], TK[NTI - 1]["xcast"]]
        pa_ctx.close()

        if "R" not in PHASES:
            p.op("sync", lambda e: None,
                 waits=tuple((k, v) for k, v in p.cnt.items()))
            emit_all(nc, p, sems)
            return nc
        pr_ctx = contextlib.ExitStack()
        p_rank = pr_ctx.enter_context(nc.psum_tensor("p_rank", [1, 2048], F32))
        p_small = pr_ctx.enter_context(nc.psum_tensor("p_small", [128, 256],
                                                      F32))

        tok_selid = [None, None]
        tok_rank_done = None
        for b in range(NB):
            sc0 = b * 16
            t_tr = p.op("tensor", lambda e, sc0=sc0: tpose(
                e, p_small[0:16, 32:160], s_pm[:, sc0:sc0 + 16], identf[:]),
                waits=tuple(phaseA_finals) + ((tok_rank_done,) if b else ()),
                inc="s_pe")
            t_r16 = p.op("scalar", lambda e: e.copy(x_sb[0:16, 0:128],
                                                    p_small[0:16, 32:160]),
                         waits=(t_tr,), inc="s_sc", drain=True)
            t_sw = p.dma("sync", lambda e, b=b: e.dma_start(
                out=srow_scr[b:b + 1, :], in_=x_sb[0:16, 0:128]),
                waits=(t_r16,), inc="r_bs")
            tok_sbc = p.dma("sync", lambda e, b=b: e.dma_start(
                out=fscr[:, 0:2048],
                in_=bass.AP(srow_scr, b * 2048, [[0, 128], [1, 2048]])),
                waits=(t_sw,), inc="r_bs")
            tok_chunk = [None] * 16
            tok_mmc = [None] * 16
            for c in range(16):
                o = (c % 2) * 2048
                od = (c % 2) * 128

                def f_cmp(e, c=c, o=o, od=od, sc0=sc0):
                    e.tensor_scalar(ubuf[:, R_gt + o:R_gt + o + 2048],
                                    fscr[:, 0:2048],
                                    s_pm[:, sc0 + c:sc0 + c + 1], None,
                                    ALU.is_lt)
                    e.tensor_scalar(ubuf[:, R_eq + o + 128 * c:R_eq + o + 2048],
                                    fscr[:, 128 * c:2048],
                                    s_pm[:, sc0 + c:sc0 + c + 1], None,
                                    ALU.is_equal)
                    e.drain()
                    return e.tensor_mul(
                        ubuf[:, R_diag + od:R_diag + od + 128],
                        ubuf[:, R_eq + o + 128 * c:R_eq + o + 128 * c + 128],
                        tril[:])
                tok_chunk[c] = p.op(
                    "vector", f_cmp,
                    waits=(tok_sbc, tok_mmc[c - 2] if c >= 2 else None),
                    inc="s_dv", drain=True)

                def f_rmm(e, c=c, o=o, od=od):
                    r = None
                    for i in range(4):
                        r = e.matmul(
                            p_rank[:, i * 512:(i + 1) * 512], onesb[:, 0:1],
                            ubuf[:, R_gt + o + i * 512:R_gt + o + (i + 1) * 512],
                            start=(c == 0), stop=(c == 15 and i != 3))
                    r0 = 128 * (c + 1)
                    while r0 < 2048:
                        r1_ = min((r0 // 512 + 1) * 512, 2048)
                        r = e.matmul(p_rank[:, r0:r1_], onesb[:, 0:1],
                                     ubuf[:, R_eq + o + r0:R_eq + o + r1_],
                                     start=False, stop=False)
                        r0 = r1_
                    return e.matmul(p_rank[:, 128 * c:128 * c + 128],
                                    onesb[:, 0:1],
                                    ubuf[:, R_diag + od:R_diag + od + 128],
                                    start=False, stop=(c == 15))
                tok_mmc[c] = p.op("tensor", f_rmm, waits=(tok_chunk[c],),
                                  inc="s_pe")
            t_rr = p.op("scalar", lambda e: e.copy(x_sb[0:1, 0:2048],
                                                   p_rank[:]),
                        waits=(tok_mmc[15],), inc="s_sc", drain=True)
            tok_dbgr = None
            if DEBUG:
                tok_dbgr = p.dma("sync", lambda e, b=b: e.dma_start(
                    out=dbg_rank[b:b + 1, :], in_=x_sb[0:1, 0:2048]),
                    waits=(t_rr,), inc="st_scr")

            def f_bt(e):
                r = None
                for c in range(16):
                    r = tpose(e, p_small[:, c:c + 1],
                              x_sb[0:1, c * 128:(c + 1) * 128],
                              identf[0:1, 0:1])
                return r
            t_bt = p.op("tensor", f_bt, waits=(t_rr,), inc="s_pe")
            t_rpm = p.op("scalar", lambda e: e.copy(rank_pm[:],
                                                    p_small[:, 0:16]),
                         waits=(t_bt,), inc="s_sc", drain=True)

            tok_pt = [None] * 16
            tok_smm = [None] * 16
            for c in range(16):
                o = (c % 2) * 512

                def f_pt(e, c=c, o=o):
                    return e.tensor_scalar(fscr[:, o:o + 512], iota512[:],
                                           rank_pm[:, c:c + 1], None,
                                           ALU.is_equal)
                tok_pt[c] = p.op(
                    "vector", f_pt,
                    waits=(t_rpm, tok_smm[c - 2] if c >= 2 else None),
                    inc="s_dv", drain=True)

                def f_smm(e, c=c, o=o):
                    r = None
                    for rc in range(4):
                        r = e.matmul(p_small[:, 16 + rc:17 + rc],
                                     fscr[:, o + rc * 128:o + (rc + 1) * 128],
                                     idvec[:, c:c + 1],
                                     start=(c == 0 and rc == 0),
                                     stop=(c == 15 and rc == 3))
                    return r
                tok_smm[c] = p.op("tensor", f_smm, waits=(tok_pt[c],),
                                  inc="s_pe")

            def f_cv(e, b=b):
                e.tensor_copy(sm[:, 0:4], p_small[:, 16:20])
                e.drain()
                return e.tensor_copy(selid[:, b * 4:b * 4 + 4], sm[:, 0:4])
            tok_selid[b] = p.op("vector", f_cv,
                                waits=(tok_smm[15], tok_dbgr),
                                inc="s_dv", drain=True)
            tok_rank_done = tok_selid[b]
            if DEBUG:
                p.dma("sync", lambda e, b=b: e.dma_start(
                    out=dbg_sel[b, :, :], in_=selid[:, b * 4:b * 4 + 4]),
                    waits=(tok_selid[b],), inc="st_scr")

        pr_ctx.close()

        if "F" not in PHASES:
            p.op("sync", lambda e: None,
                 waits=tuple((k, v) for k, v in p.cnt.items()))
            emit_all(nc, p, sems)
            return nc
        pff_ctx = contextlib.ExitStack()
        p_tp2 = pff_ctx.enter_context(nc.psum_tensor("p_tp2", [128, 2048],
                                                     BF16))
        pg = [pff_ctx.enter_context(nc.psum_tensor(f"pg{i}", [128, 512], F32))
              for i in range(3)]
        p_st = pff_ctx.enter_context(nc.psum_tensor("p_st", [128, 1024], F32))
        p_sp = pff_ctx.enter_context(nc.psum_tensor("p_sp", [128, 512], F32))

        tok_ffn_done = tok_rank_done
        tok_w2_dead = None

        def at_ap(ffc, off, width):
            if ffc < 16:
                c = F_AT1 + ffc * 512 + off
                return ubuf[:, c:c + width]
            c = 4096 + (ffc - 16) * 512 + off
            return arena2[:, c:c + width]

        for b in range(NB):
            tok_ffn_entry = tok_ffn_done
            if b == 1:
                for k in range(8):
                    tok_fwq = tload(arena2, k * 3072, fw_qkv_b, k, 3072,
                                    (tok_w2_dead, tok_ffn_done, tokd_fw1["t"])
                                    if k == 0 else tokd_fw1["t"], "t_fwq")

            g_toks = []
            for rc in range(4):
                g_toks.append(p.dma("gpsimd", lambda e, b=b, rc=rc:
                    e.indirect_dma_start(
                        out=ubuf[:, F_xsel + rc * 1024:
                                 F_xsel + (rc + 1) * 1024],
                        out_offset=None,
                        in_=ln2scr.rearrange("b l d -> (b l) d"),
                        in_offset=IndirectOffsetOnAxis(
                            selid[:, b * 4 + rc:b * 4 + rc + 1], 0),
                        element_offset=b * L * D),
                    waits=(tok_selid[b], tok_ffn_entry)
                    + last_scr_tok[b],
                    inc="ld_g" if rc == 0 else "ld_g%d" % rc))

            def f_vones(e):
                r = None
                for kc in range(4):
                    r = e.memset(bass.AP(
                        arena1, NFF_V1 + 1040 * kc + 64,
                        [[A1W, 128], [65, 16], [1, 1]]), 1.0)
                return r
            tok_vones = p.op("vector", f_vones, waits=(tok_ffn_entry,),
                             inc="s_dv", drain=True)

            tok_h1T = [None] * 4
            tok_qkvT = [None] * 4
            tok_vsel = [None] * 4
            tok_t1 = [None] * 4
            prev_hh1 = pt_chain[0]
            for rc in range(4):
                par = rc % 2
                R0 = (12, 41)[par]
                stg = F_stg2 + par * 1024
                xs = ubuf[:, F_xsel + rc * 1024:F_xsel + (rc + 1) * 1024]
                t_q = p.op("scalar", lambda e, xs=xs, R0=R0: e.activation(
                    fscr[:, 0:1024], xs, ACT.Square,
                    accum_out=sm[:, R0:R0 + 1]),
                    waits=(g_toks[rc],
                           pt_chain[0] if rc < 2 else None),
                    inc="s_sc", drain=True)
                t_rs = rstats(R0, D, EPS_RMS, (t_q,))
                t_h1 = p.op("vector", lambda e, xs=xs, R0=R0, stg=stg:
                            e.scalar_tensor_tensor(
                    ubuf[:, stg:stg + 1024], xs, sm[:, R0 + 1:R0 + 2],
                    g1bc[:], ALU.mult, ALU.mult),
                    waits=(t_rs, tok_t1[rc - 2] if rc >= 2 else None),
                    inc="s_dv", drain=True)

                def f_t1(e, par=par, stg=stg):
                    r = None
                    for j in range(8):
                        r = tpose(e, p_tp2[:, par * 1024 + j * 128:
                                           par * 1024 + (j + 1) * 128],
                                  ubuf[:, stg + j * 128:
                                       stg + (j + 1) * 128], identb[:])
                    return r
                tok_t1[rc] = p.op("tensor", f_t1,
                                  waits=(t_h1,
                                         tok_h1T[rc - 2] if rc >= 2
                                         else None), inc="s_pe")

                def f_c1(e, rc=rc, par=par):
                    r = None
                    for j in range(8):
                        r = e.copy(
                            ubuf[:, F_h1T + j * 512 + rc * 128:
                                 F_h1T + j * 512 + rc * 128 + 128],
                            p_tp2[:, par * 1024 + j * 128:
                                  par * 1024 + (j + 1) * 128])
                    return r
                tok_h1T[rc] = p.op("scalar", f_c1, waits=(tok_t1[rc],),
                                   inc="s_sc", drain=True)

                evs6 = []
                for bk in range(6):
                    nb_ = bk % 3
                    n0 = bk * 512

                    def f_mmb(e, nb_=nb_, n0=n0, rc=rc):
                        r = None
                        for k in range(8):
                            r = e.matmul(
                                pg[nb_][:, 0:512],
                                ubuf[:, F_h1T + k * 512 + rc * 128:
                                     F_h1T + k * 512 + rc * 128 + 128],
                                arena2[:, k * 3072 + n0:
                                       k * 3072 + n0 + 512],
                                start=(k == 0), stop=(k == 7))
                        return r
                    waits = [tok_h1T[rc], tok_fwq, tok_vones]
                    if bk >= 3:
                        waits.append(evs6[bk - 3])
                    if bk < 3:
                        waits.append(prev_hh1)
                    t_mmb = p.op("tensor", f_mmb, waits=tuple(waits),
                                 inc="s_pe")
                    if bk < 4:
                        ev = p.op("scalar", lambda e, bk=bk, nb_=nb_, rc=rc:
                                  e.copy(
                            ubuf[:, F_qk + rc * 2048 + bk * 512:
                                 F_qk + rc * 2048 + bk * 512 + 512],
                            pg[nb_][:, 0:512]),
                            waits=(t_mmb,), inc="s_sc", drain=True)
                    else:
                        ev = p.op("scalar", lambda e, bk=bk, nb_=nb_, rc=rc:
                                  e.copy(
                            bass.AP(arena1,
                                    NFF_V1 + 1040 * rc + 65 * 8 * (bk - 4),
                                    [[A1W, 128], [65, 8], [1, 64]]),
                            pg[nb_][:, 0:512]),
                            waits=(t_mmb,), inc="s_sc", drain=True)
                    evs6.append(ev)
                tok_hh = [evs6[2], evs6[5]]
                tok_vsel[rc] = tok_hh[1]
                tok_qkvT[rc] = tok_hh[1]
                prev_hh1 = tok_hh[1]

            wfo_tok = None
            for k in range(8):
                wfo_tok = tload(arena2, k * 1024, fw_out_b, k, 1024,
                                (tok_qkvT[3], tokd_fw2["t"])
                                if k == 0 else tokd_fw2["t"], "t_wfo")
            tok_w2 = None
            for k in range(8):
                tok_w2 = tload(arena2, 8192 + k * 2816, w2_b, k, 2816,
                               (tok_qkvT[3], tok_pad, tokd_w2["t"])
                               if k == 0 else tokd_w2["t"], "t_w2")

            kpre_ev = None
            for kc in range(3):
                def f_tkp(e, kc=kc):
                    r = None
                    for h in range(16):
                        r = tpose(e, p_tp2[0:64, h * 128:(h + 1) * 128],
                                  ubuf[:, F_qk + kc * 2048 + 1024 + h * 64:
                                       F_qk + kc * 2048 + 1024 + h * 64
                                       + 64], identb[:])
                    return r
                t_tkp = p.op("tensor", f_tkp,
                             waits=(tok_qkvT[kc], kpre_ev,
                                    tok_h1T[3] if kc == 0 else None),
                             inc="s_pe")
                kpre_ev = p.op("scalar", lambda e, kc=kc: e.copy(
                    arena1[0:64, kc * 2048:(kc + 1) * 2048],
                    p_tp2[0:64, 0:2048]),
                    waits=(t_tkp,), inc="s_sc", drain=True)

            def qTs(h):
                return ubuf[0:64, F_qkTs + h * 128:F_qkTs + h * 128 + 128]

            def kTs(h, kc=3):
                if kc < 3:
                    return arena1[0:64, kc * 2048 + h * 128:
                                  kc * 2048 + h * 128 + 128]
                return ubuf[0:64, F_qkTs + 2048 + h * 128:
                            F_qkTs + 2048 + h * 128 + 128]

            tok_r1 = [None] * 4
            tok_pg_free = None
            tok_tpe = kpre_ev
            for qc in range(4):
                def f_tq2(e, qc=qc):
                    r = None
                    for h in range(16):
                        r = tpose(e, p_tp2[0:64, h * 128:(h + 1) * 128],
                                  ubuf[:, F_qk + qc * 2048 + h * 64:
                                       F_qk + qc * 2048 + h * 64 + 64],
                                  identb[:])
                    return r
                t_tq2 = p.op("tensor", f_tq2,
                             waits=(tok_qkvT[qc], tok_h2T[3] if False else None,
                                    tok_tpe), inc="s_pe")
                tok_tqe = p.op("scalar", lambda e: e.copy(
                    ubuf[0:64, F_qkTs:F_qkTs + 2048], p_tp2[0:64, 0:2048]),
                    waits=(t_tq2,), inc="s_sc", drain=True)
                tok_tpe = tok_tqe
                tok_pv = None
                for kc in range(qc + 1):
                    tok_tke = None
                    if kc == 3:
                        def f_tk2(e, kc=kc):
                            r = None
                            for h in range(16):
                                r = tpose(e,
                                          p_tp2[0:64, h * 128:(h + 1) * 128],
                                          ubuf[:, F_qk + kc * 2048 + 1024
                                               + h * 64:
                                               F_qk + kc * 2048 + 1024
                                               + h * 64 + 64],
                                          identb[:])
                            return r
                        t_tk2 = p.op("tensor", f_tk2,
                                     waits=(tok_qkvT[kc], tok_tpe),
                                     inc="s_pe")
                        tok_tke = p.op("scalar", lambda e: e.copy(
                            ubuf[0:64, F_qkTs + 2048:F_qkTs + 4096],
                            p_tp2[0:64, 0:2048]),
                            waits=(t_tk2,), inc="s_sc", drain=True)
                        tok_tpe = tok_tke
                    tok_exp = [None, None]
                    for rnd in range(2):
                        def f_st(e, rnd=rnd, kc=kc):
                            r = None
                            for i in range(8):
                                j = (i // 2) + 4 * (i % 2)
                                h = rnd * 8 + j
                                r = e.matmul(p_st[:, j * 128:(j + 1) * 128],
                                             kTs(h, kc), qTs(h),
                                             start=True, stop=True)
                            return r
                        t_stm = p.op("tensor", f_st,
                                     waits=(tok_tke, tok_tqe, tok_pv,
                                            tok_exp[0] if rnd else None),
                                     inc="s_pe")
                        tok_exp[rnd] = p.op(
                            "scalar", lambda e, rnd=rnd: e.activation(
                                ubuf[:, F_pt + rnd * 1024:
                                     F_pt + rnd * 1024 + 1024],
                                p_st[:, 0:1024], ACT.Exp, scale=0.125),
                            waits=(t_stm,), inc="s_sc", drain=True)
                    tok_mask = tok_exp[1]
                    if kc == qc:
                        tok_mask = p.op("vector", lambda e: e.tensor_tensor(
                            ubuf[:, F_pt:F_pt + 2048],
                            ubuf[:, F_pt:F_pt + 2048],
                            bass.AP(triu, 0, [[128, 128], [0, 16], [1, 128]]),
                            ALU.mult), waits=(tok_exp[0], tok_exp[1]),
                            inc="s_dv", drain=True)

                    def f_pv(e, qc=qc, kc=kc):
                        r = None
                        for h in range(H):
                            bank, c0 = oc(h)
                            r = e.matmul(
                                pg[bank][:, c0:c0 + 65],
                                ubuf[:, F_pt + h * 128:F_pt + (h + 1) * 128],
                                arena1[:, NFF_V1 + 1040 * kc + 65 * h:
                                       NFF_V1 + 1040 * kc + 65 * h + 65],
                                start=(kc == 0 and h in (0, 7, 14)),
                                stop=(kc == qc and h in (6, 13, 15)))
                        return r
                    tok_pv = p.op("tensor", f_pv,
                                  waits=(tok_mask, tok_exp[0], tok_vsel[kc],
                                         tok_pg_free if kc == 0 else None),
                                  inc="s_pe")

                t_on = den_norm(pg, ubuf, F_o, tok_pv, 1e-30)

                def f_t3(e):
                    r = None
                    for j in range(8):
                        r = tpose(e, p_tp2[:, j * 128:(j + 1) * 128],
                                  ubuf[:, F_o + j * 128:F_o + (j + 1) * 128],
                                  identb[:])
                    return r
                t_t3 = p.op("tensor", f_t3, waits=tuple(t_on), inc="s_pe")
                t_oT = p.op("scalar", lambda e: e.copy(
                    ubuf[:, F_oT:F_oT + 1024], p_tp2[:, 0:1024]),
                    waits=(t_t3,), inc="s_sc", drain=True)

                def f_wo2(e):
                    r = None
                    for nb_ in range(2):
                        for k in range(8):
                            r = e.matmul(
                                pg[nb_][:, 0:512],
                                ubuf[:, F_oT + k * 128:F_oT + (k + 1) * 128],
                                arena2[:, k * 1024 + nb_ * 512:
                                       k * 1024 + nb_ * 512 + 512],
                                start=(k == 0), stop=(k == 7))
                    return r
                t_wo2 = p.op("tensor", f_wo2, waits=(t_oT, wfo_tok),
                             inc="s_pe")

                def f_r1(e, qc=qc):
                    r = None
                    for nb_ in range(2):
                        r = e.tensor_tensor(
                            ubuf[:, F_xsel + qc * 1024 + nb_ * 512:
                                 F_xsel + qc * 1024 + nb_ * 512 + 512],
                            pg[nb_][:, 0:512],
                            ubuf[:, F_xsel + qc * 1024 + nb_ * 512:
                                 F_xsel + qc * 1024 + nb_ * 512 + 512],
                            ALU.add)
                        e.drain()
                    return r
                tok_r1[qc] = p.op("vector", f_r1, waits=(t_wo2,), inc="s_dv",
                                  drain=True)
                tok_pg_free = tok_r1[qc]

            tok_w1 = None
            for k in range(8):
                tok_w1 = tload(arena1, k * 2816, w1_b, k, 2816,
                               (tok_r1[3],
                                (tok_selid[1] if b == 0 else None),
                                tok_pad, tokd_w1["t"])
                               if k == 0 else tokd_w1["t"], "t_w1")

            x1g_toks = []
            for rc in range(4):
                x1g_toks.append(p.dma("gpsimd", lambda e, b=b, rc=rc:
                    e.indirect_dma_start(
                        out=arena2[:, rc * 1024:(rc + 1) * 1024],
                        out_offset=None,
                        in_=x1scr.rearrange("b l d -> (b l) d"),
                        in_offset=IndirectOffsetOnAxis(
                            selid[:, b * 4 + rc:b * 4 + rc + 1], 0),
                        element_offset=b * L * D),
                    waits=(tok_r1[3],) + last_scr_tok[b],
                    inc="ld_g" if rc == 0 else "ld_g%d" % rc))

            tok_h2T = [None] * 4
            tok_t4 = [None] * 4
            for rc in range(4):
                par = rc % 2
                R0 = (12, 41)[par]
                stg = F_stg2 + par * 1024
                r1s = ubuf[:, F_xsel + rc * 1024:F_xsel + (rc + 1) * 1024]
                t_q = p.op("scalar", lambda e, r1s=r1s, R0=R0: e.activation(
                    fscr[:, 0:1024], r1s, ACT.Square,
                    accum_out=sm[:, R0:R0 + 1]),
                    waits=(tok_r1[rc],
                           tok_r1[3] if rc < 2 else None),
                    inc="s_sc", drain=True)
                t_rs = rstats(R0, D, EPS_RMS, (t_q,))
                t_h2 = p.op("vector", lambda e, r1s=r1s, R0=R0, stg=stg:
                            e.scalar_tensor_tensor(
                                ubuf[:, stg:stg + 1024], r1s,
                                sm[:, R0 + 1:R0 + 2], g2bc[:],
                                ALU.mult, ALU.mult),
                            waits=(t_rs,
                                   tok_t4[rc - 2] if rc >= 2 else None),
                            inc="s_dv", drain=True)

                def f_t4(e, par=par, stg=stg):
                    r = None
                    for j in range(8):
                        r = tpose(e, p_tp2[:, par * 1024 + j * 128:
                                           par * 1024 + (j + 1) * 128],
                                  ubuf[:, stg + j * 128:
                                       stg + (j + 1) * 128], identb[:])
                    return r
                tok_t4[rc] = p.op("tensor", f_t4,
                                  waits=(t_h2,
                                         tok_h2T[rc - 2] if rc >= 2
                                         else None), inc="s_pe")

                def f_c4(e, rc=rc, par=par):
                    r = None
                    for j in range(8):
                        r = e.copy(
                            ubuf[:, F_h1T + j * 512 + rc * 128:
                                 F_h1T + j * 512 + rc * 128 + 128],
                            p_tp2[:, par * 1024 + j * 128:
                                  par * 1024 + (j + 1) * 128])
                    return r
                tok_h2T[rc] = p.op("scalar", f_c4, waits=(tok_t4[rc],),
                                   inc="s_sc", drain=True)

            tok_a = [None] * NFFC
            for ffc in range(NFFC):
                par = ffc % 2

                def f_a(e, ffc=ffc, par=par):
                    r = None
                    for k in range(8):
                        r = e.matmul(
                            p_st[:, par * 512:par * 512 + 512],
                            arena1[:, k * 2816 + ffc * 128:
                                   k * 2816 + ffc * 128 + 128],
                            ubuf[:, F_h1T + k * 512:F_h1T + (k + 1) * 512],
                            start=(k == 0), stop=(k == 7))
                    return r
                t_a = p.op("tensor", f_a,
                           waits=(tok_h2T[3], tok_w1,
                                  tok_a[ffc - 2] if ffc >= 2 else None),
                           inc="s_pe")
                tok_a[ffc] = p.op("scalar", lambda e, ffc=ffc, par=par:
                                  e.activation(
                                      at_ap(ffc, 0, 512),
                                      p_st[:, par * 512:par * 512 + 512],
                                      ACT.Silu),
                                  waits=(t_a,), inc="s_sc", drain=True)

            tok_w3 = None
            for c in range(NFFC):
                tok_w3 = tload(arena1, c * 1024, w3_b, c, 1024,
                               (tok_a[NFFC - 1], tok_pad, tokd_w3["t"])
                               if c == 0 else tokd_w3["t"], "t_w3")

            tok_ab = [None] * NFFC
            for ffc in range(NFFC):
                par = ffc % 2

                def f_b(e, ffc=ffc, par=par):
                    r = None
                    for k in range(8):
                        r = e.matmul(
                            p_st[:, par * 512:par * 512 + 512],
                            arena2[:, 8192 + k * 2816 + ffc * 128:
                                   8192 + k * 2816 + ffc * 128 + 128],
                            ubuf[:, F_h1T + k * 512:F_h1T + (k + 1) * 512],
                            start=(k == 0), stop=(k == 7))
                    return r
                t_b = p.op("tensor", f_b,
                           waits=(tok_a[NFFC - 1], tok_w2,
                                  tok_ab[ffc - 2] if ffc >= 2 else None),
                           inc="s_pe")
                tok_ab[ffc] = p.op("vector", lambda e, ffc=ffc, par=par:
                                   e.tensor_tensor(
                                       at_ap(ffc, 0, 512),
                                       p_st[:, par * 512:par * 512 + 512],
                                       at_ap(ffc, 0, 512),
                                       ALU.mult),
                                   waits=(t_b,), inc="s_dv", drain=True)
                if b == 1 and ffc % 8 == 1:
                    emit_ptile(1, 13 + ffc // 8, 16384, 17408,
                               tuple(last_scr_tok[1]))
            tok_w2_dead = tok_ab[NFFC - 1]

            tok_scat_prev = None
            tok_xp_prev = None
            for tb in range(4):
                def f_w3(e, tb=tb):
                    r = None
                    for ffc in range(NFFC):
                        for nb_ in range(2):
                            r = e.matmul(
                                pg[nb_][:, 0:512],
                                at_ap(ffc, tb * 128, 128),
                                arena1[:, ffc * 1024 + nb_ * 512:
                                       ffc * 1024 + nb_ * 512 + 512],
                                start=(ffc == 0), stop=(ffc == NFFC - 1))
                    return r
                t_w3 = p.op("tensor", f_w3,
                            waits=(tok_ab[NFFC - 1], tok_w3, tok_xp_prev),
                            inc="s_pe")

                def f_xp(e, tb=tb):
                    r = None
                    for nb_ in range(2):
                        r = e.tensor_tensor(
                            fscr[:, nb_ * 512:nb_ * 512 + 512],
                            pg[nb_][:, 0:512],
                            ubuf[:, F_xsel + tb * 1024 + nb_ * 512:
                                 F_xsel + tb * 1024 + nb_ * 512 + 512],
                            ALU.add)
                        e.drain()
                    return r
                t_xp = p.op("vector", f_xp, waits=(t_w3,), inc="s_dv",
                            drain=True)
                tok_xp_prev = t_xp
                t_s4 = p.op("vector", lambda e: e.reduce_sum(
                    sm[:, 0:1], fscr[:, 0:1024], AX.X), waits=(t_xp,),
                    inc="s_dv", drain=True)
                t_q4 = p.op("scalar", lambda e: e.activation(
                    fscr[:, 1024:2048], fscr[:, 0:1024], ACT.Square,
                    accum_out=sm[:, 1:2]), waits=(t_xp,), inc="s_sc",
                    drain=True)
                t_st4 = stats(SB0, D, EPS_LN, (t_s4, t_q4))

                def f_os(e, tb=tb):
                    e.tensor_scalar(selstg[:], fscr[:, 0:1024], sm[:, 3:4],
                                    sm[:, 4:5], ALU.mult, ALU.add)
                    e.drain()
                    return e.tensor_tensor(
                        selstg[:], selstg[:],
                        arena2[:, tb * 1024:(tb + 1) * 1024], ALU.add)
                t_os = p.op("vector", f_os,
                            waits=(t_st4, x1g_toks[tb], tok_scat_prev),
                            inc="s_dv", drain=True)
                tok_scat_prev = p.dma("gpsimd", lambda e, b=b, tb=tb:
                    e.indirect_dma_start(
                        out=out.rearrange("b l d -> (b l) d"),
                        out_offset=IndirectOffsetOnAxis(
                            selid[:, b * 4 + tb:b * 4 + tb + 1], 0),
                        in_=selstg[:], in_offset=None,
                        element_offset=b * L * D),
                    waits=(t_os,) + tuple(pt_a_toks[b]), inc="st_sc2")
            tok_ffn_done = tok_scat_prev

        p.op("sync", lambda e: None,
             waits=tuple((k, v) for k, v in p.cnt.items()))
        pff_ctx.close()

        emit_all(nc, p, sems)

    return nc


def emit_all(nc, p, sems):
    with nc.Block() as block:
        def emit(eng_name, eng_obj):
            for fn, waits, inc, amt, drain in p.recs[eng_name]:
                for (s, v) in waits:
                    eng_obj.wait_ge(sems[s], v)
                inst = fn(eng_obj)
                if inc is not None:
                    if drain:
                        eng_obj.drain().then_inc(sems[inc], amt)
                    else:
                        assert inst is not None, (eng_name, inc)
                        inst.then_inc(sems[inc], amt)

        @block.sync
        def _(sync):
            emit("sync", sync)

        @block.tensor
        def _(tensor):
            emit("tensor", tensor)

        @block.vector
        def _(vector):
            emit("vector", vector)

        @block.scalar
        def _(scalar):
            emit("scalar", scalar)

        @block.gpsimd
        def _(gpsimd):
            emit("gpsimd", gpsimd)


_NC_CACHE = {}


def _get_nc():
    if "nc" not in _NC_CACHE:
        _NC_CACHE["nc"] = build_program()
    return _NC_CACHE["nc"]


def kernel(L0_attn_x, L0_attn_w_qkv, L0_attn_w_out, L0_ffn_w_router,
           L0_ffn_W_qkv, L0_ffn_W_out, L0_ffn_g1, L0_ffn_g2, L0_ffn_W1,
           L0_ffn_W2, L0_ffn_W3):
    x = np.ascontiguousarray(np.asarray(L0_attn_x, dtype=np.float32))
    consts = dict(
        c_identf=np.eye(128, dtype=np.float32),
        c_identb=np.eye(128, dtype=np.float32).astype(ml_dtypes.bfloat16),
        c_triu=(np.arange(128)[:, None] <= np.arange(128)[None, :]
                ).astype(ml_dtypes.bfloat16),
        c_tril=(np.arange(128)[:, None] < np.arange(128)[None, :]
                ).astype(ml_dtypes.bfloat16),
        c_iota512=np.tile(np.arange(512, dtype=np.float32)[None, :],
                          (128, 1)),
        c_idvec=(np.arange(128, dtype=np.float32)[:, None]
                 + 128.0 * np.arange(16, dtype=np.float32)[None, :]),
        c_onesb=np.ones((128, 1), np.float32).astype(ml_dtypes.bfloat16),
    )
    wr = np.asarray(L0_ffn_w_router, dtype=np.float32)
    weights = dict(
        aw_qkv=np.ascontiguousarray(np.asarray(L0_attn_w_qkv, np.float32)),
        aw_out=np.ascontiguousarray(np.asarray(L0_attn_w_out, np.float32)),
        fw_qkv=np.ascontiguousarray(np.asarray(L0_ffn_W_qkv, np.float32)),
        fw_out=np.ascontiguousarray(np.asarray(L0_ffn_W_out, np.float32)),
        w1=np.ascontiguousarray(np.asarray(L0_ffn_W1, np.float32)),
        w2=np.ascontiguousarray(np.asarray(L0_ffn_W2, np.float32)),
        w3=np.ascontiguousarray(np.asarray(L0_ffn_W3, np.float32)),
        wr_in=np.ascontiguousarray(np.tile(wr[None, :], (128, 1))),
        g1_in=np.ascontiguousarray(
            np.tile(np.asarray(L0_ffn_g1, np.float32)[None, :],
                    (128, 1))).astype(ml_dtypes.bfloat16),
        g2_in=np.ascontiguousarray(
            np.tile(np.asarray(L0_ffn_g2, np.float32)[None, :],
                    (128, 1))).astype(ml_dtypes.bfloat16),
    )

    nc = _get_nc()
    in_maps = []
    for core in range(8):
        m = dict(weights)
        m.update(consts)
        m["x_in"] = np.ascontiguousarray(x[2 * core:2 * core + 2])
        in_maps.append(m)
    res = run_bass_kernel_spmd(nc, in_maps, core_ids=list(range(8)))
    return np.concatenate([r["out"] for r in res.results], axis=0)


# version 23
# speedup vs baseline: 1.0232x; 1.0232x over previous
import contextlib
import numpy as np
import ml_dtypes
import concourse.bass as bass
import concourse.mybir as mybir
from concourse.bass_utils import run_bass_kernel_spmd
from concourse.bass_core_rust import IndirectOffsetOnAxis

F32 = mybir.dt.float32
BF16 = mybir.dt.bfloat16
U32 = mybir.dt.uint32
ALU = mybir.AluOpType
ACT = mybir.ActivationFunctionType
AX = mybir.AxisListType

D = 1024
L = 2048
T = 16
H = 16
DFF = 2730
DFFP = 2816
NFFC = 22
NB = 2
EPS_LN = 1e-5
EPS_RMS = 1e-6
EPS_DEN = 1e-6

ENGINES = ("sync", "tensor", "vector", "scalar", "gpsimd")
DEBUG = False
PHASES = "ARF"
STOP_STAGE = 99


class Plan:
    def __init__(self):
        self.recs = {e: [] for e in ENGINES}
        self.cnt = {}

    def op(self, eng, fn, waits=(), inc=None, inc_amt=1, drain=False):
        tok = None
        if inc is not None:
            self.cnt[inc] = self.cnt.get(inc, 0) + inc_amt
            tok = (inc, self.cnt[inc])
        self.recs[eng].append((fn, tuple(w for w in waits if w is not None),
                               inc, inc_amt, drain))
        return tok

    def dma(self, eng, fn, waits=(), inc=None):
        return self.op(eng, fn, waits, inc, inc_amt=16)


def build_program():
    nc = bass.Bass("TRN2", target_bir_lowering=False)

    x_in = nc.dram_tensor("x_in", [NB, L, D], F32, kind="ExternalInput")
    aw_qkv = nc.dram_tensor("aw_qkv", [3 * D, D], F32, kind="ExternalInput")
    aw_out = nc.dram_tensor("aw_out", [D, D], F32, kind="ExternalInput")
    fw_qkv = nc.dram_tensor("fw_qkv", [3 * D, D], F32, kind="ExternalInput")
    fw_out = nc.dram_tensor("fw_out", [D, D], F32, kind="ExternalInput")
    w1 = nc.dram_tensor("w1", [DFF, D], F32, kind="ExternalInput")
    w2 = nc.dram_tensor("w2", [DFF, D], F32, kind="ExternalInput")
    w3 = nc.dram_tensor("w3", [D, DFF], F32, kind="ExternalInput")
    wr_in = nc.dram_tensor("wr_in", [128, D], F32, kind="ExternalInput")
    g1_in = nc.dram_tensor("g1_in", [128, D], BF16, kind="ExternalInput")
    g2_in = nc.dram_tensor("g2_in", [128, D], BF16, kind="ExternalInput")
    c_identf = nc.dram_tensor("c_identf", [128, 128], F32, kind="ExternalInput")
    c_identb = nc.dram_tensor("c_identb", [128, 128], BF16, kind="ExternalInput")
    c_triu = nc.dram_tensor("c_triu", [128, 128], BF16, kind="ExternalInput")
    c_tril = nc.dram_tensor("c_tril", [128, 128], BF16, kind="ExternalInput")
    c_iota512 = nc.dram_tensor("c_iota512", [128, 512], F32, kind="ExternalInput")
    c_idvec = nc.dram_tensor("c_idvec", [128, 16], F32, kind="ExternalInput")
    c_onesb = nc.dram_tensor("c_onesb", [128, 1], BF16, kind="ExternalInput")

    out = nc.dram_tensor("out", [NB, L, D], F32, kind="ExternalOutput")
    if DEBUG:
        dbg_x1 = nc.dram_tensor("dbg_x1", [NB, L, D], F32, kind="ExternalOutput")
        dbg_rank = nc.dram_tensor("dbg_rank", [NB, 2048], F32,
                                  kind="ExternalOutput")
        dbg_sel = nc.dram_tensor("dbg_sel", [NB, 128, 4], U32,
                                 kind="ExternalOutput")

    aw_qkv_b = nc.dram_tensor("aw_qkv_b", [3 * D, D], BF16)
    aw_out_b = nc.dram_tensor("aw_out_b", [D, D], BF16)
    fw_qkv_b = nc.dram_tensor("fw_qkv_b", [3 * D, D], BF16)
    fw_out_b = nc.dram_tensor("fw_out_b", [D, D], BF16)
    w1_b = nc.dram_tensor("w1_b", [DFFP, D], BF16)
    w2_b = nc.dram_tensor("w2_b", [DFFP, D], BF16)
    w3_b = nc.dram_tensor("w3_b", [D, DFFP], BF16)
    x1scr = nc.dram_tensor("x1scr", [NB, L, D], BF16)
    srow_scr = nc.dram_tensor("srow_scr", [NB, 2048], F32)
    ln2scr = nc.dram_tensor("ln2scr", [NB, L, D], BF16)

    p = Plan()

    with contextlib.ExitStack() as ctx:
        def sbt(name, shape, dt):
            return ctx.enter_context(nc.sbuf_tensor(name, shape, dt))

        arena1 = sbt("arena1", [128, 32768], BF16)
        arena2 = sbt("arena2", [128, 30720], BF16)
        ubuf = sbt("ubuf", [128, 18432], BF16)
        x_sb = sbt("x_sb", [128, 3072], F32)
        fscr = sbt("fscr", [128, 2048], F32)
        fscr2 = sbt("fscr2", [128, 1536], BF16)
        ptstg = sbt("ptstg", [128, 1024], F32)
        selstg = sbt("selstg", [128, 1024], F32)
        wrbc = sbt("wrbc", [128, 1024], F32)
        g1bc = sbt("g1bc", [128, 1024], BF16)
        g2bc = sbt("g2bc", [128, 1024], BF16)
        identf = sbt("identf", [128, 128], F32)
        identb = sbt("identb", [128, 128], BF16)
        triu = sbt("triu", [128, 128], BF16)
        tril = sbt("tril", [128, 128], BF16)
        iota512 = sbt("iota512", [128, 512], F32)
        idvec = sbt("idvec", [128, 16], F32)
        onesb = sbt("onesb", [128, 1], BF16)
        S_sb = sbt("S_sb", [64, 1040], BF16)
        S_sb_holder = [S_sb]
        s_pm = sbt("s_pm", [128, 32], F32)
        selid = sbt("selid", [128, 8], U32)
        sumw = sbt("sumw", [128, 1], F32)
        sm = sbt("sm", [128, 56], F32)
        rank_pm = sbt("rank_pm", [128, 16], F32)
        zero16 = sbt("zero16", [128, 1024], BF16)

        UW = ubuf.shape[1]
        A1W = arena1.shape[1]
        SB0, SBP = 0, 33

        U_xln, U_xlnT, U_qk, U_v1, U_qkT, U_at, U_o, U_oT, U_stg = \
            0, 1024, 2048, 4096, 5184, 9280, 11328, 12352, 13376
        R_gt, R_eq, R_diag = 0, 4096, 8192
        F_xsel = 0
        F_h1T = 4096
        F_qkTs = 4096
        F_qk = 8192
        F_AT1 = 8192
        F_pt = 16384
        F_o, F_oT = 16384, 17408
        F_stg2 = 16384
        NFF_V1 = 22528

        sems = {n: ctx.enter_context(nc.semaphore(n)) for n in
                ["ld_c", "ld_x", "ld_xa", "ld_x1", "ld_x2", "ld_g", "t_awo", "st_scr", "st_out",
                 "s_pe", "s_dv", "s_sc",
                 "c_aw1", "c_aw2", "c_fw1", "c_fw2", "c_w1", "c_w2", "c_w3",
                 "c_pad", "st_sc2", "st_xc", "st_dbg", "ld_p0", "ld_p1", "ld_g1", "ld_g2", "ld_g3",
                 "t_aw", "t_fwq", "t_wfo", "t_w2", "t_w1", "t_w3", "r_bs"]}

        def tpose(e, dst, src, ident):
            return e.matmul(dst, src, ident, start=True, stop=True,
                            is_transpose=True)

        tok_c = None
        for dst, src in ((wrbc[:], wr_in[:, :]), (g1bc[:], g1_in[:, :]),
                         (g2bc[:], g2_in[:, :]), (identf[:], c_identf[:, :]),
                         (identb[:], c_identb[:, :]), (triu[:], c_triu[:, :]),
                         (tril[:], c_tril[:, :]), (iota512[:], c_iota512[:, :]),
                         (idvec[:], c_idvec[:, :]),
                         (onesb[:], c_onesb[:, :])):
            tok_c = p.dma("sync", lambda e, d=dst, s=src:
                          e.dma_start(out=d, in_=s), inc="ld_c")

        def cast_weight(dst, src, rows, cols, sem, rl=512):
            tok = None
            r = 0
            while r < rows:
                r2 = min(r + rl, rows)
                tok = p.dma("gpsimd", lambda e, d=dst, s=src, a=r, b2=r2:
                            e.dma_start(out=d[a:b2, 0:cols],
                                        in_=s[a:b2, 0:cols]),
                            inc=sem)
                r = r2
            return tok

        tok_cast_aw1 = cast_weight(aw_qkv_b, aw_qkv, 3 * D, D, "c_aw1")
        tok_cast_aw2 = cast_weight(aw_out_b, aw_out, D, D, "c_aw2")
        p.op("vector", lambda e: e.memset(zero16[:], 0.0))
        tok_z = p.op("vector", lambda e: None, (), "s_dv", 1, drain=True)
        p.dma("gpsimd", lambda e: e.dma_start(out=w1_b[DFF:DFFP, :],
                                              in_=zero16[0:DFFP - DFF, :]),
              waits=(tok_z,), inc="c_pad")
        p.dma("gpsimd", lambda e: e.dma_start(out=w2_b[DFF:DFFP, :],
                                              in_=zero16[0:DFFP - DFF, :]),
              waits=(tok_z,), inc="c_pad")
        tok_pad = None
        for i in range(8):
            tok_pad = p.dma("gpsimd", lambda e, i=i: e.dma_start(
                out=w3_b[i * 128:(i + 1) * 128, DFF:DFFP],
                in_=zero16[0:128, 0:DFFP - DFF]), waits=(tok_z,), inc="c_pad")
        deferred_casts = []

        def defer_cast(dst, src, rows, cols, sem, rl=512):
            toks = {}
            r = 0
            while r < rows:
                r2 = min(r + rl, rows)
                def emit1(a=r, b2=r2, dst=dst, src=src, cols=cols, sem=sem,
                          toks=toks):
                    toks["t"] = p.dma(
                        "gpsimd", lambda e, d=dst, s=src, a=a, b2=b2:
                        e.dma_start(out=d[a:b2, 0:cols],
                                    in_=s[a:b2, 0:cols]), inc=sem)
                deferred_casts.append(emit1)
                r = r2
            return toks

        tokd_fw1 = defer_cast(fw_qkv_b, fw_qkv, 3 * D, D, "c_fw1")
        tokd_fw2 = defer_cast(fw_out_b, fw_out, D, D, "c_fw2")
        tokd_w1 = defer_cast(w1_b, w1, DFF, D, "c_w1")
        tokd_w2 = defer_cast(w2_b, w2, DFF, D, "c_w2")
        tokd_w3 = defer_cast(w3_b, w3, D, DFF, "c_w3")
        n_fw1_chunks = 6

        def tload(arena, col, src_b, kchunk, width, wait, sem):
            if wait is None:
                wl = ()
            elif isinstance(wait, tuple) and len(wait) == 2 \
                    and isinstance(wait[0], str):
                wl = (wait,)
            else:
                wl = tuple(wait)
            return p.dma("sync", lambda e, a=arena, c=col, s=src_b, k=kchunk,
                         w=width: e.dma_start_transpose(
                             a[:, c:c + w], s[:, k * 128:(k + 1) * 128]),
                         waits=wl, inc=sem)

        for k in range(8):
            tok_wa = tload(arena1, k * 3072, aw_qkv_b, k, 3072, tok_cast_aw1,
                           "t_aw")
        tok_awo = None
        for k in range(8):
            tok_awo = tload(arena1, 24576 + k * 1024, aw_out_b, k, 1024,
                            tok_cast_aw2, "t_awo")

        p.op("vector", lambda e: e.reduce_sum(sumw[:], wrbc[:], AX.X),
             waits=(tok_c,))
        p.op("vector", lambda e: e.memset(
            bass.AP(ubuf, U_v1 + 64, [[UW, 128], [65, 16], [1, 1]]), 1.0))
        p.op("vector", lambda e: e.memset(S_sb_holder[0][:], 0.0))
        tok_ones = p.op("vector", lambda e: None, (), "s_dv", 1, drain=True)

        def stats(c0, n, eps, toks):
            def f1(e):
                e.tensor_scalar(sm[:, c0 + 2:c0 + 3], sm[:, c0:c0 + 1],
                                1.0 / n, None, ALU.mult)
                e.drain()
                e.tensor_tensor(sm[:, c0 + 5:c0 + 6], sm[:, c0 + 2:c0 + 3],
                                sm[:, c0 + 2:c0 + 3], ALU.mult)
                e.drain()
                e.scalar_tensor_tensor(sm[:, c0 + 6:c0 + 7],
                                       sm[:, c0 + 1:c0 + 2], 1.0 / n,
                                       sm[:, c0 + 5:c0 + 6], ALU.mult,
                                       ALU.subtract)
                e.drain()
                return e.tensor_scalar(sm[:, c0 + 6:c0 + 7],
                                       sm[:, c0 + 6:c0 + 7], float(eps),
                                       None, ALU.add)
            t1 = p.op("vector", f1, waits=toks[1:], inc="s_dv", drain=True)

            def f2s(e):
                if c0 == 0:
                    e.activation(sm[:, c0 + 7:c0 + 8], sm[:, c0 + 6:c0 + 7],
                                 ACT.Ln)
                elif c0 == 8:
                    e.activation(sm[:, c0 + 7:c0 + 8], sm[:, c0 + 6:c0 + 7],
                                 ACT.Ln)
                elif c0 == 41:
                    e.activation(sm[:, c0 + 7:c0 + 8], sm[:, c0 + 6:c0 + 7],
                                 ACT.Ln)
                else:
                    e.activation(sm[:, c0 + 7:c0 + 8], sm[:, c0 + 6:c0 + 7],
                                 ACT.Ln)
                e.drain()
                return e.activation(sm[:, c0 + 3:c0 + 4],
                                    sm[:, c0 + 7:c0 + 8], ACT.Exp,
                                    scale=-0.5)
            t2 = p.op("scalar", f2s, waits=(t1,), inc="s_sc", drain=True)
            return p.op("vector", lambda e: e.scalar_tensor_tensor(
                sm[:, c0 + 4:c0 + 5], sm[:, c0 + 2:c0 + 3], -1.0,
                sm[:, c0 + 3:c0 + 4], ALU.mult, ALU.mult),
                waits=(t2,), inc="s_dv", drain=True)

        def rstats(c0, n, eps, toks):
            t1 = p.op("vector", lambda e: e.tensor_scalar(
                sm[:, c0 + 2:c0 + 3], sm[:, c0:c0 + 1], 1.0 / n, float(eps),
                ALU.mult, ALU.add), waits=toks, inc="s_dv", drain=True)

            def f2s(e):
                e.activation(sm[:, c0 + 3:c0 + 4], sm[:, c0 + 2:c0 + 3],
                             ACT.Ln)
                e.drain()
                return e.activation(sm[:, c0 + 1:c0 + 2],
                                    sm[:, c0 + 3:c0 + 4], ACT.Exp,
                                    scale=-0.5)
            return p.op("scalar", f2s, waits=(t1,), inc="s_sc", drain=True)

        HBANKS = ((0, 7), (7, 7), (14, 2))

        def oc(h):
            return h // 7, 65 * (h % 7)

        def den_norm(pb, out_tensor, out_col, wait_tok, eps):
            def f_den(e):
                for m, (h0, nh) in enumerate(HBANKS):
                    src = pb[m][:, 0:65 * nh].rearrange(
                        "p (j e) -> p j e", e=65)[:, :, 64:65]
                    e.tensor_scalar_max(sm[:, 16 + h0:16 + h0 + nh], src,
                                        float(eps))
                    e.drain()
                return e.reciprocal(sm[:, 16:32], sm[:, 16:32])
            t_dn = p.op("vector", f_den, waits=(wait_tok,), inc="s_dv",
                        drain=True)

            on_toks = []
            for m, (h0, nh) in enumerate(HBANKS):
                def f_on(e, m=m, h0=h0, nh=nh):
                    src = pb[m][:, 0:65 * nh].rearrange(
                        "p (j e) -> p j e", e=65)[:, :, 0:64]
                    dst = out_tensor[:, out_col + 64 * h0:
                                     out_col + 64 * (h0 + nh)].rearrange(
                        "p (j e) -> p j e", e=64)
                    rec = bass.AP(sm, 16 + h0,
                                  [[sm.shape[1], 128], [1, nh],
                                   [0, 64]])
                    return e.tensor_tensor(dst, src, rec, ALU.mult)
                on_toks.append(p.op("vector", f_on, waits=(),
                                    inc="s_dv", drain=True))
            return on_toks

        pa_ctx = contextlib.ExitStack()
        p_tpb = pa_ctx.enter_context(nc.psum_tensor("p_tpb", [128, 2048], BF16))
        pf = [pa_ctx.enter_context(nc.psum_tensor(f"pf{i}", [128, 512], F32))
              for i in range(3)]
        p_S = pa_ctx.enter_context(nc.psum_tensor("p_S", [64, 1536], F32))

        tiles = [(b, t) for b in range(NB) for t in range(T)]
        NTI = len(tiles)
        TK = [dict() for _ in range(NTI)]

        def emit_xloadA(i):
            b, t = tiles[i]
            d = TK[i]
            xb = i % 3
            xfree = TK[i - 3]["x1"] if i >= 3 else None
            d["x"] = p.dma("scalar", lambda e, b=b, t=t, xb=xb: e.dma_start(
                out=x_sb[:, xb * 1024:xb * 1024 + 1024],
                in_=x_in[b, t * 128:(t + 1) * 128, :]),
                waits=(xfree,), inc=("ld_xa", "ld_x1", "ld_x2")[xb])

        def emit_headA(i):
            d = TK[i]
            xb = i % 3
            d["s"] = p.op("vector", lambda e, xb=xb: e.reduce_sum(
                sm[:, 0:1], x_sb[:, xb * 1024:xb * 1024 + 1024], AX.X),
                waits=(d["x"],), inc="s_dv", drain=True)
            d["q"] = p.op("scalar", lambda e, xb=xb: e.activation(
                ubuf[:, U_stg:U_stg + 1024],
                x_sb[:, xb * 1024:xb * 1024 + 1024], ACT.Square,
                accum_out=sm[:, 1:2]), waits=(d["x"],), inc="s_sc",
                drain=True)
            t_st = stats(SB0, D, EPS_LN, (d["s"], d["q"]))
            d["xln"] = p.op("scalar", lambda e, xb=xb: e.activation(
                ubuf[:, U_xln:U_xln + 1024],
                x_sb[:, xb * 1024:xb * 1024 + 1024], ACT.Identity,
                bias=sm[:, 4:5], scale=sm[:, 3:4]),
                waits=(t_st,), inc="s_sc", drain=True)

        def emit_midA_pre(i):
            d = TK[i]
            def f_tp(e):
                r = None
                for j in range(8):
                    r = tpose(e, p_tpb[:, j * 128:(j + 1) * 128],
                              ubuf[:, U_xln + j * 128:U_xln + (j + 1) * 128],
                              identb[:])
                return r
            d["tp"] = p.op("tensor", f_tp,
                           waits=(d["xln"],
                                  TK[i - 1]["oT"] if i else None)
                           + ((tok_c,) if i == 0 else ()), inc="s_pe")
            d["xlnT"] = p.op("scalar", lambda e: e.copy(
                ubuf[:, U_xlnT:U_xlnT + 1024], p_tpb[:, 0:1024]),
                waits=(d["tp"],), inc="s_sc", drain=True)
            d["phi"] = []
            d["v"] = []

        def emit_bankA(i, bk):
            d = TK[i]
            nb_ = bk % 3
            n0 = bk * 512

            def f_mmb(e, nb_=nb_, n0=n0):
                r = None
                for k in range(8):
                    r = e.matmul(
                        pf[nb_][:, 0:512],
                        ubuf[:, U_xlnT + k * 128:U_xlnT + (k + 1) * 128],
                        arena1[:, k * 3072 + n0:k * 3072 + n0 + 512],
                        start=(k == 0), stop=(k == 7))
                return r
            waits = [d["xlnT"]]
            if bk >= 3:
                waits.append(d["phi"][bk - 3])
            if bk < 2 and i > 0:
                waits.append(TK[i - 1]["aoev"])
            if i == 0 and bk == 0:
                waits.append(tok_wa)
            t_mmb = p.op("tensor", f_mmb, waits=tuple(waits), inc="s_pe")
            d["mm%d" % bk] = t_mmb
            if bk < 4:
                t_minb = p.op("vector", lambda e, nb_=nb_:
                              e.tensor_scalar_min(
                                  fscr2[:, nb_ * 512:nb_ * 512 + 512],
                                  pf[nb_][:, 0:512], 0.0),
                              waits=(t_mmb,), inc="s_dv", drain=True)
                t_expb = p.op("scalar", lambda e, nb_=nb_: e.activation(
                    fscr2[:, nb_ * 512:nb_ * 512 + 512],
                    fscr2[:, nb_ * 512:nb_ * 512 + 512], ACT.Exp),
                    waits=(t_minb,), inc="s_sc", drain=True)
                t_phib = p.op("vector", lambda e, bk=bk, nb_=nb_:
                              e.scalar_tensor_tensor(
                                  ubuf[:, U_qk + bk * 512:
                                       U_qk + bk * 512 + 512],
                                  pf[nb_][:, 0:512], 0.0,
                                  fscr2[:, nb_ * 512:nb_ * 512 + 512],
                                  ALU.max, ALU.add),
                              waits=(t_expb,), inc="s_dv", drain=True)
                d["phi"].append(t_phib)
            else:
                t_vb = p.op("vector", lambda e, bk=bk, nb_=nb_:
                            e.tensor_copy(
                                bass.AP(ubuf, U_v1 + 65 * 8 * (bk - 4),
                                        [[UW, 128], [65, 8], [1, 64]]),
                                pf[nb_][:, 0:512]), waits=(t_mmb,),
                            inc="s_dv", drain=True)
                d["v"].append(t_vb)

        def emit_tqA(i):
            d = TK[i]
            RND = ((0, 0, 0), (16, 2048, 1024), (8, 1024, 0), (24, 3072, 1024))
            PHI = (0, 2, 1, 3)
            evs = []
            for r, (s0, dcol, pbase) in enumerate(RND):
                def f_tq(e, s0=s0, pbase=pbase):
                    r_ = None
                    for j in range(8):
                        r_ = tpose(e,
                                   p_tpb[0:64, pbase + j * 128:
                                         pbase + (j + 1) * 128],
                                   ubuf[:, U_qk + (s0 + j) * 64:
                                        U_qk + (s0 + j) * 64 + 64],
                                   identb[:])
                    return r_
                t_tq = p.op("tensor", f_tq,
                            waits=(d["phi"][PHI[r]],
                                   evs[r - 2] if r >= 2 else None),
                            inc="s_pe")
                ev = p.op("scalar", lambda e, dcol=dcol, pbase=pbase: e.copy(
                    ubuf[0:64, U_qkT + dcol:U_qkT + dcol + 1024],
                    p_tpb[0:64, pbase:pbase + 1024]),
                    waits=(t_tq,), inc="s_sc", drain=True)
                evs.append(ev)
            d["qkT"] = evs

        def emit_midA_post(i):
            b, t = tiles[i]
            d = TK[i]
            t_atm = None
            am_toks = []
            for rnd in range(2):
                def f_at(e, rnd=rnd):
                    r = None
                    for j in range(8):
                        jj = (j // 2) + 4 * (j % 2)
                        h = rnd * 8 + jj
                        r = e.matmul(pf[jj // 4][:, (jj % 4) * 128:
                                                 (jj % 4) * 128 + 128],
                                     kT_ap(h), qT_ap(h),
                                     start=True, stop=True)
                    return r
                at_w = (d["qkT"][0], d["qkT"][1], d["phi"][3], d["v"][0]) \
                    if rnd == 0 else (d["qkT"][2], d["qkT"][3], t_atm)
                t_at = p.op("tensor", f_at, waits=at_w, inc="s_pe")

                def f_am(e, rnd=rnd):
                    r = None
                    for j in range(2):
                        r = e.tensor_tensor(
                            ubuf[:, U_at + rnd * 1024 + j * 512:
                                 U_at + rnd * 1024 + j * 512 + 512],
                            pf[j][:, 0:512],
                            bass.AP(triu, 0,
                                    [[128, 128], [0, 4], [1, 128]]),
                            ALU.mult)
                    return r
                t_atm = p.op("vector", f_am, waits=(t_at,), inc="s_dv",
                             drain=True)
                am_toks.append(t_atm)

            def f_o(e, t=t):
                r = None
                for h in range(H):
                    bank, c0 = oc(h)
                    if t > 0:
                        r = e.matmul(
                            pf[bank][:, c0:c0 + 65], qT_ap(h),
                            S_sb[:, 65 * h:65 * h + 65],
                            start=(h in (0, 7, 14)), stop=False)
                    r = e.matmul(
                        pf[bank][:, c0:c0 + 65],
                        ubuf[:, U_at + h * 128:U_at + (h + 1) * 128],
                        ubuf[:, U_v1 + 65 * h:U_v1 + 65 * h + 65],
                        start=(t == 0 and h in (0, 7, 14)),
                        stop=(h in (6, 13, 15)))
                return r
            t_o = p.op("tensor", f_o,
                       waits=(t_atm, d["v"][1],
                              TK[i - 1]["scp"] if i else None)
                       + ((tok_ones,) if i == 0 else ()),
                       inc="s_pe")

            def f_kv(e, t=t):
                r = None
                for h in range(H):
                    bank, c0 = oc(h)
                    r = e.matmul(
                        p_S[:, 512 * bank + c0:512 * bank + c0 + 65],
                        ubuf[:, U_qk + 1024 + h * 64:
                             U_qk + 1024 + (h + 1) * 64],
                        ubuf[:, U_v1 + 65 * h:U_v1 + 65 * h + 65],
                        start=(t == 0 and h in (0, 7, 14)),
                        stop=(h in (6, 13, 15)),
                        skip_group_check=(t > 0))
                return r
            t_kv = p.op("tensor", f_kv,
                        waits=(t_atm, d["v"][1])
                        + ((tok_ones,) if i == 0 else ()), inc="s_pe")

            on_toks = den_norm(pf, ubuf, U_o, t_o, EPS_DEN)
            t_ob = on_toks[2]
            t_to = None
            for j0, j1, w in ((0, 3, (on_toks[0], d["qkT"][2])),
                              (3, 7, (on_toks[1],)),
                              (7, 8, (on_toks[2],))):
                t_to = p.op("tensor", lambda e, j0=j0, j1=j1: [
                    tpose(e, p_tpb[:, j * 128:(j + 1) * 128],
                          ubuf[:, U_o + j * 128:U_o + (j + 1) * 128],
                          identb[:]) for j in range(j0, j1)][-1],
                    waits=w + (t_to,), inc="s_pe")

            def f_scp(e):
                r = None
                for m, (h0, nh) in enumerate(HBANKS):
                    r = e.copy(S_sb[:, 65 * h0:65 * (h0 + nh)],
                               p_S[:, 512 * m:512 * m + 65 * nh])
                    e.drain()
                return r
            d["scp"] = p.op("scalar", f_scp, waits=(t_kv,), inc="s_sc",
                            drain=True)
            d["oT"] = p.op("scalar", lambda e: e.copy(
                ubuf[:, U_oT:U_oT + 1024], p_tpb[:, 0:1024]),
                waits=(t_to,), inc="s_sc", drain=True)

        def emit_woA(i):
            d = TK[i]

            def f_wo(e):
                r = None
                for nb_ in range(2):
                    for k in range(8):
                        r = e.matmul(
                            pf[nb_][:, 0:512],
                            ubuf[:, U_oT + k * 128:U_oT + (k + 1) * 128],
                            arena1[:, 24576 + k * 1024 + nb_ * 512:
                                   24576 + k * 1024 + nb_ * 512 + 512],
                            start=(k == 0), stop=(k == 7))
                return r
            d["wo"] = p.op("tensor", f_wo,
                           waits=(d["oT"],)
                           + ((tok_awo,) if i == 0 else ()), inc="s_pe")

        def emit_aoevA(j):
            d = TK[j]

            def f_aoev(e):
                e.copy(fscr[:, 0:512], pf[0][:, 0:512])
                e.drain()
                return e.copy(fscr[:, 512:1024], pf[1][:, 0:512])
            d["aoev"] = p.op("scalar", f_aoev,
                             waits=(d["wo"],
                                    TK[j - 1]["x1"] if j else None),
                             inc="s_sc", drain=True)

        def emit_tailA1(j):
            d = TK[j]
            d["s2"] = p.op("vector", lambda e: e.reduce_sum(
                sm[:, 8:9], fscr[:, 0:1024], AX.X), waits=(d["aoev"],),
                inc="s_dv", drain=True)
            d["q2"] = p.op("scalar", lambda e: e.activation(
                ubuf[:, U_stg:U_stg + 1024], fscr[:, 0:1024], ACT.Square,
                accum_out=sm[:, 9:10]), waits=(d["aoev"],), inc="s_sc",
                drain=True)

        def emit_tailA2(j):
            d = TK[j]
            d["st2"] = stats(8, D, EPS_LN, (d["s2"], d["q2"]))

        def emit_tailA3(j):
            d = TK[j]

            def f_x1(e, j=j):
                e.tensor_scalar(fscr[:, 0:1024], fscr[:, 0:1024],
                                sm[:, 11:12], sm[:, 12:13],
                                ALU.mult, ALU.add)
                e.drain()
                xb = j % 3
                return e.tensor_tensor(fscr[:, 1024:2048], fscr[:, 0:1024],
                                       x_sb[:, xb * 1024:xb * 1024 + 1024],
                                       ALU.add)
            d["x1"] = p.op("vector", f_x1,
                           waits=(TK[j - 1]["stg"] if j else None,
                                  TK[j - 1]["xcast"] if j else None),
                           inc="s_dv", drain=True)
            b_, t_ = tiles[j]
            d["xcast"] = p.dma("gpsimd", lambda e, b=b_, t=t_: e.dma_start(
                out=x1scr[b, t * 128:(t + 1) * 128, :],
                in_=fscr[:, 1024:2048]), waits=(d["x1"],), inc="st_xc")
            t_s3 = p.op("vector", lambda e: e.reduce_sum(
                sm[:, 41:42], fscr[:, 1024:2048], AX.X), waits=(),
                inc="s_dv", drain=True)
            d["q3"] = p.op("scalar", lambda e: e.activation(
                ubuf[:, U_stg:U_stg + 1024], fscr[:, 1024:2048], ACT.Square,
                accum_out=sm[:, 42:43]), waits=(d["x1"],), inc="s_sc",
                drain=True)
            d["st3"] = stats(41, D, EPS_LN, (t_s3, d["q3"]))

        def emit_tailA4(j):
            b, t = tiles[j]
            d = TK[j]

            def f_sc(e, b=b, t=t):
                e.tensor_tensor(selstg[:], fscr[:, 1024:2048],
                                wrbc[:], ALU.mult)
                e.drain()
                e.reduce_sum(sm[:, 49:50], selstg[:], AX.X)
                e.drain()
                e.tensor_tensor(sm[:, 50:51], sm[:, 43:44], sumw[:],
                                ALU.mult)
                e.drain()
                e.tensor_tensor(sm[:, 49:50], sm[:, 49:50], sm[:, 50:51],
                                ALU.subtract)
                e.drain()
                return e.tensor_tensor(s_pm[:, b * 16 + t:b * 16 + t + 1],
                                       sm[:, 49:50], sm[:, 44:45],
                                       ALU.mult)
            d["sc"] = p.op("vector", f_sc, waits=(d["q3"],),
                           inc="s_dv", drain=True)

            def f_stg(e):
                return e.activation(ubuf[:, U_stg + 1024:U_stg + 2048],
                                    fscr[:, 1024:2048], ACT.Identity,
                                    bias=sm[:, 45:46], scale=sm[:, 44:45])
            d["stg"] = p.op("scalar", f_stg,
                            waits=(d["st3"],
                                   TK[j - 1]["stgfree"] if j else None),
                            inc="s_sc", drain=True)
            d["stgfree"] = p.dma("sync", lambda e, b=b, t=t: e.dma_start(
                out=ln2scr[b, t * 128:(t + 1) * 128, :],
                in_=ubuf[:, U_stg + 1024:U_stg + 2048]), waits=(d["stg"],),
                inc="st_scr")
            if DEBUG:
                d["xcast"] = p.dma("sync", lambda e, b=b, t=t:
                                   e.dma_start(
                    out=dbg_x1[b, t * 128:(t + 1) * 128, :],
                    in_=fscr[:, 1024:2048]), waits=(d["sc"], d["stg"],
                                                    d["xcast"]),
                    inc="st_dbg")

        pt_a_toks = {0: [], 1: []}
        pt_chain = [None]
        PTS = [(0, t) for t in range(T)] + [(1, t) for t in range(13)]
        PTK = [dict() for _ in PTS]

        def emit_ptile(b, t, ycol, xcol, extra_waits=()):
            emit_pt_ld(b, t, ycol, xcol, None, extra_waits)
            emit_pt_sc(-1)
            emit_pt_v(-1)

        _lastpt = [None]

        def emit_pt_ld(b, t, ycol, xcol, k, extra_waits=()):
            d = dict(b=b, t=t, ycol=ycol, xcol=xcol)
            d["ly"] = p.dma("scalar", lambda e, b=b, t=t, ycol=ycol:
                            e.dma_start(
                out=ubuf[:, ycol:ycol + 1024],
                in_=ln2scr[b, t * 128:(t + 1) * 128, :]),
                waits=(pt_chain[0],) + tuple(extra_waits), inc="ld_p0")
            d["lx"] = p.dma("scalar", lambda e, b=b, t=t, xcol=xcol:
                            e.dma_start(
                out=ubuf[:, xcol:xcol + 1024],
                in_=x1scr[b, t * 128:(t + 1) * 128, :]),
                waits=(pt_chain[0],) + tuple(extra_waits), inc="ld_p1")
            if k is not None:
                PTK[k].update(d)
            _lastpt[0] = d

        def emit_pt_sc(k):
            d = PTK[k] if k >= 0 else _lastpt[0]
            ycol = d["ycol"]
            t_q = p.op("scalar", lambda e, ycol=ycol: e.activation(
                ptstg[:], ubuf[:, ycol:ycol + 1024], ACT.Square,
                accum_out=sm[:, SBP + 1:SBP + 2]), waits=(d["ly"],),
                inc="s_sc", drain=True)
            t_f1 = p.op("vector", lambda e: e.tensor_scalar(
                sm[:, SBP + 2:SBP + 3], sm[:, SBP + 1:SBP + 2],
                1.0 / D, float(EPS_LN), ALU.mult, ALU.add),
                waits=(t_q,), inc="s_dv", drain=True)

            def f2s(e):
                e.activation(sm[:, SBP + 4:SBP + 5], sm[:, SBP + 2:SBP + 3],
                             ACT.Ln)
                e.drain()
                return e.activation(sm[:, SBP + 3:SBP + 4],
                                    sm[:, SBP + 4:SBP + 5], ACT.Exp,
                                    scale=-0.5)
            d["ps"] = p.op("scalar", f2s, waits=(t_f1,), inc="s_sc",
                           drain=True)

        def emit_pt_v(k):
            d = PTK[k] if k >= 0 else _lastpt[0]
            ycol, xcol, b, t = d["ycol"], d["xcol"], d["b"], d["t"]

            def f_out(e, ycol=ycol, xcol=xcol):
                e.tensor_scalar(ptstg[:], ubuf[:, ycol:ycol + 1024],
                                sm[:, SBP + 3:SBP + 4], None, ALU.mult)
                e.drain()
                return e.tensor_tensor(ptstg[:], ptstg[:],
                                       ubuf[:, xcol:xcol + 1024], ALU.add)
            t_out = p.op("vector", f_out, waits=(d["ps"], d["lx"]),
                         inc="s_dv", drain=True)
            tok_st = p.dma("sync", lambda e, b=b, t=t: e.dma_start(
                out=out[b, t * 128:(t + 1) * 128, :], in_=ptstg[:]),
                waits=(t_out,), inc="st_out")
            pt_chain[0] = tok_st
            pt_a_toks[b].append(tok_st)

        def qT_ap(h):
            return ubuf[0:64, U_qkT + h * 128:U_qkT + h * 128 + 128]

        def kT_ap(h):
            return ubuf[0:64, U_qkT + 2048 + h * 128:
                        U_qkT + 2048 + h * 128 + 128]

        emit_xloadA(0)
        emit_xloadA(1)
        emit_headA(0)
        emit_midA_pre(0)
        for i in range(NTI):
            emit_bankA(i, 0)
            if 0 <= i - 3 < len(PTS):
                emit_pt_sc(i - 3)
            if i >= 1:
                emit_tailA1(i - 1)
            for bk in range(1, 4):
                emit_bankA(i, bk)
            if i >= 1:
                emit_tailA2(i - 1)
            emit_bankA(i, 4)
            emit_bankA(i, 5)
            if 0 <= i - 3 < len(PTS):
                emit_pt_v(i - 3)
            emit_tqA(i)
            if i + 1 < NTI:
                emit_headA(i + 1)
            emit_midA_post(i)
            if i >= 1:
                emit_tailA3(i - 1)
                if deferred_casts:
                    deferred_casts.pop(0)()
                    n_fw1_chunks -= 1
                    if n_fw1_chunks == 0:
                        for k in range(8):
                            tok_fwq = tload(arena2, k * 3072, fw_qkv_b, k,
                                            3072, tokd_fw1["t"], "t_fwq")
            if i + 1 < NTI:
                emit_midA_pre(i + 1)
            emit_woA(i)
            emit_aoevA(i)
            if i + 2 < NTI:
                emit_xloadA(i + 2)
            if 0 <= i - 2 < len(PTS):
                bpt, tpt = PTS[i - 2]
                src = TK[i - 2]
                emit_pt_ld(bpt, tpt, 15424, 16448, i - 2,
                           (src["stgfree"], src["xcast"]))
            if i >= 1:
                emit_tailA4(i - 1)
        emit_tailA1(NTI - 1)
        emit_tailA2(NTI - 1)
        emit_tailA3(NTI - 1)
        emit_tailA4(NTI - 1)

        last_scr_tok = [(TK[T - 1]["stgfree"], TK[T - 1]["xcast"]),
                        (TK[NTI - 1]["stgfree"], TK[NTI - 1]["xcast"])]
        phaseA_finals = [TK[NTI - 1]["stgfree"], TK[NTI - 1]["scp"],
                         TK[NTI - 1]["sc"], TK[NTI - 1]["xcast"]]
        pa_ctx.close()

        if "R" not in PHASES:
            p.op("sync", lambda e: None,
                 waits=tuple((k, v) for k, v in p.cnt.items()))
            emit_all(nc, p, sems)
            return nc
        pr_ctx = contextlib.ExitStack()
        p_rank = pr_ctx.enter_context(nc.psum_tensor("p_rank", [1, 2048], F32))
        p_small = pr_ctx.enter_context(nc.psum_tensor("p_small", [128, 256],
                                                      F32))

        tok_selid = [None, None]
        tok_rank_done = None
        for b in range(NB):
            sc0 = b * 16
            t_tr = p.op("tensor", lambda e, sc0=sc0: tpose(
                e, p_small[0:16, 32:160], s_pm[:, sc0:sc0 + 16], identf[:]),
                waits=tuple(phaseA_finals) + ((tok_rank_done,) if b else ()),
                inc="s_pe")
            t_r16 = p.op("scalar", lambda e: e.copy(x_sb[0:16, 0:128],
                                                    p_small[0:16, 32:160]),
                         waits=(t_tr,), inc="s_sc", drain=True)
            t_sw = p.dma("sync", lambda e, b=b: e.dma_start(
                out=srow_scr[b:b + 1, :], in_=x_sb[0:16, 0:128]),
                waits=(t_r16,), inc="r_bs")
            tok_sbc = p.dma("sync", lambda e, b=b: e.dma_start(
                out=fscr[:, 0:2048],
                in_=bass.AP(srow_scr, b * 2048, [[0, 128], [1, 2048]])),
                waits=(t_sw,), inc="r_bs")
            tok_chunk = [None] * 16
            tok_mmc = [None] * 16
            for c in range(16):
                o = (c % 2) * 2048
                od = (c % 2) * 128

                def f_cmp(e, c=c, o=o, od=od, sc0=sc0):
                    e.tensor_scalar(ubuf[:, R_gt + o:R_gt + o + 2048],
                                    fscr[:, 0:2048],
                                    s_pm[:, sc0 + c:sc0 + c + 1], None,
                                    ALU.is_lt)
                    e.tensor_scalar(ubuf[:, R_eq + o + 128 * c:R_eq + o + 2048],
                                    fscr[:, 128 * c:2048],
                                    s_pm[:, sc0 + c:sc0 + c + 1], None,
                                    ALU.is_equal)
                    e.drain()
                    return e.tensor_mul(
                        ubuf[:, R_diag + od:R_diag + od + 128],
                        ubuf[:, R_eq + o + 128 * c:R_eq + o + 128 * c + 128],
                        tril[:])
                tok_chunk[c] = p.op(
                    "vector", f_cmp,
                    waits=(tok_sbc, tok_mmc[c - 2] if c >= 2 else None),
                    inc="s_dv", drain=True)

                def f_rmm(e, c=c, o=o, od=od):
                    r = None
                    for i in range(4):
                        r = e.matmul(
                            p_rank[:, i * 512:(i + 1) * 512], onesb[:, 0:1],
                            ubuf[:, R_gt + o + i * 512:R_gt + o + (i + 1) * 512],
                            start=(c == 0), stop=(c == 15 and i != 3))
                    r0 = 128 * (c + 1)
                    while r0 < 2048:
                        r1_ = min((r0 // 512 + 1) * 512, 2048)
                        r = e.matmul(p_rank[:, r0:r1_], onesb[:, 0:1],
                                     ubuf[:, R_eq + o + r0:R_eq + o + r1_],
                                     start=False, stop=False)
                        r0 = r1_
                    return e.matmul(p_rank[:, 128 * c:128 * c + 128],
                                    onesb[:, 0:1],
                                    ubuf[:, R_diag + od:R_diag + od + 128],
                                    start=False, stop=(c == 15))
                tok_mmc[c] = p.op("tensor", f_rmm, waits=(tok_chunk[c],),
                                  inc="s_pe")
            t_rr = p.op("scalar", lambda e: e.copy(x_sb[0:1, 0:2048],
                                                   p_rank[:]),
                        waits=(tok_mmc[15],), inc="s_sc", drain=True)
            tok_dbgr = None
            if DEBUG:
                tok_dbgr = p.dma("sync", lambda e, b=b: e.dma_start(
                    out=dbg_rank[b:b + 1, :], in_=x_sb[0:1, 0:2048]),
                    waits=(t_rr,), inc="st_scr")

            def f_bt(e):
                r = None
                for c in range(16):
                    r = tpose(e, p_small[:, c:c + 1],
                              x_sb[0:1, c * 128:(c + 1) * 128],
                              identf[0:1, 0:1])
                return r
            t_bt = p.op("tensor", f_bt, waits=(t_rr,), inc="s_pe")
            t_rpm = p.op("scalar", lambda e: e.copy(rank_pm[:],
                                                    p_small[:, 0:16]),
                         waits=(t_bt,), inc="s_sc", drain=True)

            tok_pt = [None] * 16
            tok_smm = [None] * 16
            for c in range(16):
                o = (c % 2) * 512

                def f_pt(e, c=c, o=o):
                    return e.tensor_scalar(fscr[:, o:o + 512], iota512[:],
                                           rank_pm[:, c:c + 1], None,
                                           ALU.is_equal)
                tok_pt[c] = p.op(
                    "vector", f_pt,
                    waits=(t_rpm, tok_smm[c - 2] if c >= 2 else None),
                    inc="s_dv", drain=True)

                def f_smm(e, c=c, o=o):
                    r = None
                    for rc in range(4):
                        r = e.matmul(p_small[:, 16 + rc:17 + rc],
                                     fscr[:, o + rc * 128:o + (rc + 1) * 128],
                                     idvec[:, c:c + 1],
                                     start=(c == 0 and rc == 0),
                                     stop=(c == 15 and rc == 3))
                    return r
                tok_smm[c] = p.op("tensor", f_smm, waits=(tok_pt[c],),
                                  inc="s_pe")

            def f_cv(e, b=b):
                e.tensor_copy(sm[:, 0:4], p_small[:, 16:20])
                e.drain()
                return e.tensor_copy(selid[:, b * 4:b * 4 + 4], sm[:, 0:4])
            tok_selid[b] = p.op("vector", f_cv,
                                waits=(tok_smm[15], tok_dbgr),
                                inc="s_dv", drain=True)
            tok_rank_done = tok_selid[b]
            if DEBUG:
                p.dma("sync", lambda e, b=b: e.dma_start(
                    out=dbg_sel[b, :, :], in_=selid[:, b * 4:b * 4 + 4]),
                    waits=(tok_selid[b],), inc="st_scr")

        pr_ctx.close()

        if "F" not in PHASES:
            p.op("sync", lambda e: None,
                 waits=tuple((k, v) for k, v in p.cnt.items()))
            emit_all(nc, p, sems)
            return nc
        pff_ctx = contextlib.ExitStack()
        p_tp2 = pff_ctx.enter_context(nc.psum_tensor("p_tp2", [128, 2048],
                                                     BF16))
        pg = [pff_ctx.enter_context(nc.psum_tensor(f"pg{i}", [128, 512], F32))
              for i in range(3)]
        p_st = pff_ctx.enter_context(nc.psum_tensor("p_st", [128, 1024], F32))
        p_sp = pff_ctx.enter_context(nc.psum_tensor("p_sp", [128, 512], F32))

        tok_ffn_done = tok_rank_done
        tok_w2_dead = None

        def at_ap(ffc, off, width):
            if ffc < 16:
                c = F_AT1 + ffc * 512 + off
                return ubuf[:, c:c + width]
            c = 4096 + (ffc - 16) * 512 + off
            return arena2[:, c:c + width]

        for b in range(NB):
            tok_ffn_entry = tok_ffn_done
            if b == 1:
                for k in range(8):
                    tok_fwq = tload(arena2, k * 3072, fw_qkv_b, k, 3072,
                                    (tok_w2_dead, tok_ffn_done, tokd_fw1["t"])
                                    if k == 0 else tokd_fw1["t"], "t_fwq")

            g_toks = []
            for rc in range(4):
                g_toks.append(p.dma("gpsimd", lambda e, b=b, rc=rc:
                    e.indirect_dma_start(
                        out=ubuf[:, F_xsel + rc * 1024:
                                 F_xsel + (rc + 1) * 1024],
                        out_offset=None,
                        in_=ln2scr.rearrange("b l d -> (b l) d"),
                        in_offset=IndirectOffsetOnAxis(
                            selid[:, b * 4 + rc:b * 4 + rc + 1], 0),
                        element_offset=b * L * D),
                    waits=(tok_selid[b], tok_ffn_entry)
                    + last_scr_tok[b],
                    inc="ld_g" if rc == 0 else "ld_g%d" % rc))

            def f_vones(e):
                r = None
                for kc in range(4):
                    r = e.memset(bass.AP(
                        arena1, NFF_V1 + 1040 * kc + 64,
                        [[A1W, 128], [65, 16], [1, 1]]), 1.0)
                return r
            tok_vones = p.op("vector", f_vones, waits=(tok_ffn_entry,),
                             inc="s_dv", drain=True)

            tok_h1T = [None] * 4
            tok_qkvT = [None] * 4
            tok_vsel = [None] * 4
            tok_t1 = [None] * 4
            prev_hh1 = pt_chain[0]
            for rc in range(4):
                par = rc % 2
                R0 = (12, 41)[par]
                stg = F_stg2 + par * 1024
                xs = ubuf[:, F_xsel + rc * 1024:F_xsel + (rc + 1) * 1024]
                t_q = p.op("scalar", lambda e, xs=xs, R0=R0: e.activation(
                    fscr[:, 0:1024], xs, ACT.Square,
                    accum_out=sm[:, R0:R0 + 1]),
                    waits=(g_toks[rc],
                           pt_chain[0] if rc < 2 else None),
                    inc="s_sc", drain=True)
                t_rs = rstats(R0, D, EPS_RMS, (t_q,))
                t_h1 = p.op("vector", lambda e, xs=xs, R0=R0, stg=stg:
                            e.scalar_tensor_tensor(
                    ubuf[:, stg:stg + 1024], xs, sm[:, R0 + 1:R0 + 2],
                    g1bc[:], ALU.mult, ALU.mult),
                    waits=(t_rs, tok_t1[rc - 2] if rc >= 2 else None),
                    inc="s_dv", drain=True)

                def f_t1(e, par=par, stg=stg):
                    r = None
                    for j in range(8):
                        r = tpose(e, p_tp2[:, par * 1024 + j * 128:
                                           par * 1024 + (j + 1) * 128],
                                  ubuf[:, stg + j * 128:
                                       stg + (j + 1) * 128], identb[:])
                    return r
                tok_t1[rc] = p.op("tensor", f_t1,
                                  waits=(t_h1,
                                         tok_h1T[rc - 2] if rc >= 2
                                         else None), inc="s_pe")

                def f_c1(e, rc=rc, par=par):
                    r = None
                    for j in range(8):
                        r = e.copy(
                            ubuf[:, F_h1T + j * 512 + rc * 128:
                                 F_h1T + j * 512 + rc * 128 + 128],
                            p_tp2[:, par * 1024 + j * 128:
                                  par * 1024 + (j + 1) * 128])
                    return r
                tok_h1T[rc] = p.op("scalar", f_c1, waits=(tok_t1[rc],),
                                   inc="s_sc", drain=True)

                evs6 = []
                for bk in range(6):
                    nb_ = bk % 3
                    n0 = bk * 512

                    def f_mmb(e, nb_=nb_, n0=n0, rc=rc):
                        r = None
                        for k in range(8):
                            r = e.matmul(
                                pg[nb_][:, 0:512],
                                ubuf[:, F_h1T + k * 512 + rc * 128:
                                     F_h1T + k * 512 + rc * 128 + 128],
                                arena2[:, k * 3072 + n0:
                                       k * 3072 + n0 + 512],
                                start=(k == 0), stop=(k == 7))
                        return r
                    waits = [tok_h1T[rc], tok_fwq, tok_vones]
                    if bk >= 3:
                        waits.append(evs6[bk - 3])
                    if bk < 3:
                        waits.append(prev_hh1)
                    t_mmb = p.op("tensor", f_mmb, waits=tuple(waits),
                                 inc="s_pe")
                    if bk < 4:
                        ev = p.op("scalar", lambda e, bk=bk, nb_=nb_, rc=rc:
                                  e.copy(
                            ubuf[:, F_qk + rc * 2048 + bk * 512:
                                 F_qk + rc * 2048 + bk * 512 + 512],
                            pg[nb_][:, 0:512]),
                            waits=(t_mmb,), inc="s_sc", drain=True)
                    else:
                        ev = p.op("scalar", lambda e, bk=bk, nb_=nb_, rc=rc:
                                  e.copy(
                            bass.AP(arena1,
                                    NFF_V1 + 1040 * rc + 65 * 8 * (bk - 4),
                                    [[A1W, 128], [65, 8], [1, 64]]),
                            pg[nb_][:, 0:512]),
                            waits=(t_mmb,), inc="s_sc", drain=True)
                    evs6.append(ev)
                tok_hh = [evs6[2], evs6[5]]
                tok_vsel[rc] = tok_hh[1]
                tok_qkvT[rc] = tok_hh[1]
                prev_hh1 = tok_hh[1]

            wfo_tok = None
            for k in range(8):
                wfo_tok = tload(arena2, k * 1024, fw_out_b, k, 1024,
                                (tok_qkvT[3], tokd_fw2["t"])
                                if k == 0 else tokd_fw2["t"], "t_wfo")
            tok_w2 = None
            for k in range(8):
                tok_w2 = tload(arena2, 8192 + k * 2816, w2_b, k, 2816,
                               (tok_qkvT[3], tok_pad, tokd_w2["t"])
                               if k == 0 else tokd_w2["t"], "t_w2")

            kpre_ev = None
            for kc in range(3):
                def f_tkp(e, kc=kc):
                    r = None
                    for h in range(16):
                        r = tpose(e, p_tp2[0:64, h * 128:(h + 1) * 128],
                                  ubuf[:, F_qk + kc * 2048 + 1024 + h * 64:
                                       F_qk + kc * 2048 + 1024 + h * 64
                                       + 64], identb[:])
                    return r
                t_tkp = p.op("tensor", f_tkp,
                             waits=(tok_qkvT[kc], kpre_ev,
                                    tok_h1T[3] if kc == 0 else None),
                             inc="s_pe")
                kpre_ev = p.op("scalar", lambda e, kc=kc: e.copy(
                    arena1[0:64, kc * 2048:(kc + 1) * 2048],
                    p_tp2[0:64, 0:2048]),
                    waits=(t_tkp,), inc="s_sc", drain=True)

            def qTs(h):
                return ubuf[0:64, F_qkTs + h * 128:F_qkTs + h * 128 + 128]

            def kTs(h, kc=3):
                if kc < 3:
                    return arena1[0:64, kc * 2048 + h * 128:
                                  kc * 2048 + h * 128 + 128]
                return ubuf[0:64, F_qkTs + 2048 + h * 128:
                            F_qkTs + 2048 + h * 128 + 128]

            tok_r1 = [None] * 4
            tok_pg_free = None
            tok_tpe = kpre_ev
            for qc in range(4):
                def f_tq2(e, qc=qc):
                    r = None
                    for h in range(16):
                        r = tpose(e, p_tp2[0:64, h * 128:(h + 1) * 128],
                                  ubuf[:, F_qk + qc * 2048 + h * 64:
                                       F_qk + qc * 2048 + h * 64 + 64],
                                  identb[:])
                    return r
                t_tq2 = p.op("tensor", f_tq2,
                             waits=(tok_qkvT[qc], tok_h2T[3] if False else None,
                                    tok_tpe), inc="s_pe")
                tok_tqe = p.op("scalar", lambda e: e.copy(
                    ubuf[0:64, F_qkTs:F_qkTs + 2048], p_tp2[0:64, 0:2048]),
                    waits=(t_tq2,), inc="s_sc", drain=True)
                tok_tpe = tok_tqe
                tok_pv = None
                for kc in range(qc + 1):
                    tok_tke = None
                    if kc == 3:
                        def f_tk2(e, kc=kc):
                            r = None
                            for h in range(16):
                                r = tpose(e,
                                          p_tp2[0:64, h * 128:(h + 1) * 128],
                                          ubuf[:, F_qk + kc * 2048 + 1024
                                               + h * 64:
                                               F_qk + kc * 2048 + 1024
                                               + h * 64 + 64],
                                          identb[:])
                            return r
                        t_tk2 = p.op("tensor", f_tk2,
                                     waits=(tok_qkvT[kc], tok_tpe),
                                     inc="s_pe")
                        tok_tke = p.op("scalar", lambda e: e.copy(
                            ubuf[0:64, F_qkTs + 2048:F_qkTs + 4096],
                            p_tp2[0:64, 0:2048]),
                            waits=(t_tk2,), inc="s_sc", drain=True)
                        tok_tpe = tok_tke
                    tok_exp = [None, None]
                    for rnd in range(2):
                        def f_st(e, rnd=rnd, kc=kc):
                            r = None
                            for i in range(8):
                                j = (i // 2) + 4 * (i % 2)
                                h = rnd * 8 + j
                                r = e.matmul(p_st[:, j * 128:(j + 1) * 128],
                                             kTs(h, kc), qTs(h),
                                             start=True, stop=True)
                            return r
                        t_stm = p.op("tensor", f_st,
                                     waits=(tok_tke, tok_tqe, tok_pv,
                                            tok_exp[0] if rnd else None),
                                     inc="s_pe")
                        tok_exp[rnd] = p.op(
                            "scalar", lambda e, rnd=rnd: e.activation(
                                ubuf[:, F_pt + rnd * 1024:
                                     F_pt + rnd * 1024 + 1024],
                                p_st[:, 0:1024], ACT.Exp, scale=0.125),
                            waits=(t_stm,), inc="s_sc", drain=True)
                    tok_mask = tok_exp[1]
                    if kc == qc:
                        tok_mask = p.op("vector", lambda e: e.tensor_tensor(
                            ubuf[:, F_pt:F_pt + 2048],
                            ubuf[:, F_pt:F_pt + 2048],
                            bass.AP(triu, 0, [[128, 128], [0, 16], [1, 128]]),
                            ALU.mult), waits=(tok_exp[0], tok_exp[1]),
                            inc="s_dv", drain=True)

                    def f_pv(e, qc=qc, kc=kc):
                        r = None
                        for h in range(H):
                            bank, c0 = oc(h)
                            r = e.matmul(
                                pg[bank][:, c0:c0 + 65],
                                ubuf[:, F_pt + h * 128:F_pt + (h + 1) * 128],
                                arena1[:, NFF_V1 + 1040 * kc + 65 * h:
                                       NFF_V1 + 1040 * kc + 65 * h + 65],
                                start=(kc == 0 and h in (0, 7, 14)),
                                stop=(kc == qc and h in (6, 13, 15)))
                        return r
                    tok_pv = p.op("tensor", f_pv,
                                  waits=(tok_mask, tok_exp[0], tok_vsel[kc],
                                         tok_pg_free if kc == 0 else None),
                                  inc="s_pe")

                t_on = den_norm(pg, ubuf, F_o, tok_pv, 1e-30)

                def f_t3(e):
                    r = None
                    for j in range(8):
                        r = tpose(e, p_tp2[:, j * 128:(j + 1) * 128],
                                  ubuf[:, F_o + j * 128:F_o + (j + 1) * 128],
                                  identb[:])
                    return r
                t_t3 = p.op("tensor", f_t3, waits=tuple(t_on), inc="s_pe")
                t_oT = p.op("scalar", lambda e: e.copy(
                    ubuf[:, F_oT:F_oT + 1024], p_tp2[:, 0:1024]),
                    waits=(t_t3,), inc="s_sc", drain=True)

                def f_wo2(e):
                    r = None
                    for nb_ in range(2):
                        for k in range(8):
                            r = e.matmul(
                                pg[nb_][:, 0:512],
                                ubuf[:, F_oT + k * 128:F_oT + (k + 1) * 128],
                                arena2[:, k * 1024 + nb_ * 512:
                                       k * 1024 + nb_ * 512 + 512],
                                start=(k == 0), stop=(k == 7))
                    return r
                t_wo2 = p.op("tensor", f_wo2, waits=(t_oT, wfo_tok),
                             inc="s_pe")

                def f_r1(e, qc=qc):
                    r = None
                    for nb_ in range(2):
                        r = e.tensor_tensor(
                            ubuf[:, F_xsel + qc * 1024 + nb_ * 512:
                                 F_xsel + qc * 1024 + nb_ * 512 + 512],
                            pg[nb_][:, 0:512],
                            ubuf[:, F_xsel + qc * 1024 + nb_ * 512:
                                 F_xsel + qc * 1024 + nb_ * 512 + 512],
                            ALU.add)
                        e.drain()
                    return r
                tok_r1[qc] = p.op("vector", f_r1, waits=(t_wo2,), inc="s_dv",
                                  drain=True)
                tok_pg_free = tok_r1[qc]

            tok_w1 = None
            for k in range(8):
                tok_w1 = tload(arena1, k * 2816, w1_b, k, 2816,
                               (tok_r1[3],
                                (tok_selid[1] if b == 0 else None),
                                tok_pad, tokd_w1["t"])
                               if k == 0 else tokd_w1["t"], "t_w1")

            x1g_toks = []
            for rc in range(4):
                x1g_toks.append(p.dma("gpsimd", lambda e, b=b, rc=rc:
                    e.indirect_dma_start(
                        out=arena2[:, rc * 1024:(rc + 1) * 1024],
                        out_offset=None,
                        in_=x1scr.rearrange("b l d -> (b l) d"),
                        in_offset=IndirectOffsetOnAxis(
                            selid[:, b * 4 + rc:b * 4 + rc + 1], 0),
                        element_offset=b * L * D),
                    waits=(tok_r1[3],) + last_scr_tok[b],
                    inc="ld_g" if rc == 0 else "ld_g%d" % rc))

            tok_h2T = [None] * 4
            tok_t4 = [None] * 4
            for rc in range(4):
                par = rc % 2
                R0 = (12, 41)[par]
                stg = F_stg2 + par * 1024
                r1s = ubuf[:, F_xsel + rc * 1024:F_xsel + (rc + 1) * 1024]
                t_q = p.op("scalar", lambda e, r1s=r1s, R0=R0: e.activation(
                    fscr[:, 0:1024], r1s, ACT.Square,
                    accum_out=sm[:, R0:R0 + 1]),
                    waits=(tok_r1[rc],
                           tok_r1[3] if rc < 2 else None),
                    inc="s_sc", drain=True)
                t_rs = rstats(R0, D, EPS_RMS, (t_q,))
                t_h2 = p.op("vector", lambda e, r1s=r1s, R0=R0, stg=stg:
                            e.scalar_tensor_tensor(
                                ubuf[:, stg:stg + 1024], r1s,
                                sm[:, R0 + 1:R0 + 2], g2bc[:],
                                ALU.mult, ALU.mult),
                            waits=(t_rs,
                                   tok_t4[rc - 2] if rc >= 2 else None),
                            inc="s_dv", drain=True)

                def f_t4(e, par=par, stg=stg):
                    r = None
                    for j in range(8):
                        r = tpose(e, p_tp2[:, par * 1024 + j * 128:
                                           par * 1024 + (j + 1) * 128],
                                  ubuf[:, stg + j * 128:
                                       stg + (j + 1) * 128], identb[:])
                    return r
                tok_t4[rc] = p.op("tensor", f_t4,
                                  waits=(t_h2,
                                         tok_h2T[rc - 2] if rc >= 2
                                         else None), inc="s_pe")

                def f_c4(e, rc=rc, par=par):
                    r = None
                    for j in range(8):
                        r = e.copy(
                            ubuf[:, F_h1T + j * 512 + rc * 128:
                                 F_h1T + j * 512 + rc * 128 + 128],
                            p_tp2[:, par * 1024 + j * 128:
                                  par * 1024 + (j + 1) * 128])
                    return r
                tok_h2T[rc] = p.op("scalar", f_c4, waits=(tok_t4[rc],),
                                   inc="s_sc", drain=True)

            tok_a = [None] * NFFC
            for ffc in range(NFFC):
                par = ffc % 2

                def f_a(e, ffc=ffc, par=par):
                    r = None
                    for k in range(8):
                        r = e.matmul(
                            p_st[:, par * 512:par * 512 + 512],
                            arena1[:, k * 2816 + ffc * 128:
                                   k * 2816 + ffc * 128 + 128],
                            ubuf[:, F_h1T + k * 512:F_h1T + (k + 1) * 512],
                            start=(k == 0), stop=(k == 7))
                    return r
                t_a = p.op("tensor", f_a,
                           waits=(tok_h2T[3], tok_w1,
                                  tok_a[ffc - 2] if ffc >= 2 else None),
                           inc="s_pe")
                tok_a[ffc] = p.op("scalar", lambda e, ffc=ffc, par=par:
                                  e.activation(
                                      at_ap(ffc, 0, 512),
                                      p_st[:, par * 512:par * 512 + 512],
                                      ACT.Silu),
                                  waits=(t_a,), inc="s_sc", drain=True)

            tok_w3 = None
            for c in range(NFFC):
                tok_w3 = tload(arena1, c * 1024, w3_b, c, 1024,
                               (tok_a[NFFC - 1], tok_pad, tokd_w3["t"])
                               if c == 0 else tokd_w3["t"], "t_w3")

            tok_ab = [None] * NFFC
            for ffc in range(NFFC):
                par = ffc % 2

                def f_b(e, ffc=ffc, par=par):
                    r = None
                    for k in range(8):
                        r = e.matmul(
                            p_st[:, par * 512:par * 512 + 512],
                            arena2[:, 8192 + k * 2816 + ffc * 128:
                                   8192 + k * 2816 + ffc * 128 + 128],
                            ubuf[:, F_h1T + k * 512:F_h1T + (k + 1) * 512],
                            start=(k == 0), stop=(k == 7))
                    return r
                t_b = p.op("tensor", f_b,
                           waits=(tok_a[NFFC - 1], tok_w2,
                                  tok_ab[ffc - 2] if ffc >= 2 else None),
                           inc="s_pe")
                tok_ab[ffc] = p.op("vector", lambda e, ffc=ffc, par=par:
                                   e.tensor_tensor(
                                       at_ap(ffc, 0, 512),
                                       p_st[:, par * 512:par * 512 + 512],
                                       at_ap(ffc, 0, 512),
                                       ALU.mult),
                                   waits=(t_b,), inc="s_dv", drain=True)
                if b == 1 and ffc % 8 == 1:
                    emit_ptile(1, 13 + ffc // 8, 16384, 17408,
                               tuple(last_scr_tok[1]))
            tok_w2_dead = tok_ab[NFFC - 1]

            tok_scat_prev = None
            tok_xp_prev = None
            for tb in range(4):
                def f_w3(e, tb=tb):
                    r = None
                    for ffc in range(NFFC):
                        for nb_ in range(2):
                            r = e.matmul(
                                pg[nb_][:, 0:512],
                                at_ap(ffc, tb * 128, 128),
                                arena1[:, ffc * 1024 + nb_ * 512:
                                       ffc * 1024 + nb_ * 512 + 512],
                                start=(ffc == 0), stop=(ffc == NFFC - 1))
                    return r
                t_w3 = p.op("tensor", f_w3,
                            waits=(tok_ab[NFFC - 1], tok_w3, tok_xp_prev),
                            inc="s_pe")

                def f_xp(e, tb=tb):
                    r = None
                    for nb_ in range(2):
                        r = e.tensor_tensor(
                            fscr[:, nb_ * 512:nb_ * 512 + 512],
                            pg[nb_][:, 0:512],
                            ubuf[:, F_xsel + tb * 1024 + nb_ * 512:
                                 F_xsel + tb * 1024 + nb_ * 512 + 512],
                            ALU.add)
                        e.drain()
                    return r
                t_xp = p.op("vector", f_xp, waits=(t_w3,), inc="s_dv",
                            drain=True)
                tok_xp_prev = t_xp
                t_s4 = p.op("vector", lambda e: e.reduce_sum(
                    sm[:, 0:1], fscr[:, 0:1024], AX.X), waits=(),
                    inc="s_dv", drain=True)
                t_q4 = p.op("scalar", lambda e: e.activation(
                    fscr[:, 1024:2048], fscr[:, 0:1024], ACT.Square,
                    accum_out=sm[:, 1:2]), waits=(t_xp,), inc="s_sc",
                    drain=True)
                t_st4 = stats(SB0, D, EPS_LN, (t_s4, t_q4))

                def f_os(e, tb=tb):
                    e.tensor_scalar(selstg[:], fscr[:, 0:1024], sm[:, 3:4],
                                    sm[:, 4:5], ALU.mult, ALU.add)
                    e.drain()
                    return e.tensor_tensor(
                        selstg[:], selstg[:],
                        arena2[:, tb * 1024:(tb + 1) * 1024], ALU.add)
                t_os = p.op("vector", f_os,
                            waits=(x1g_toks[tb], tok_scat_prev),
                            inc="s_dv", drain=True)
                tok_scat_prev = p.dma("gpsimd", lambda e, b=b, tb=tb:
                    e.indirect_dma_start(
                        out=out.rearrange("b l d -> (b l) d"),
                        out_offset=IndirectOffsetOnAxis(
                            selid[:, b * 4 + tb:b * 4 + tb + 1], 0),
                        in_=selstg[:], in_offset=None,
                        element_offset=b * L * D),
                    waits=(t_os,) + tuple(pt_a_toks[b]), inc="st_sc2")
            tok_ffn_done = tok_scat_prev

        p.op("sync", lambda e: None,
             waits=tuple((k, v) for k, v in p.cnt.items()))
        pff_ctx.close()

        emit_all(nc, p, sems)

    return nc


def emit_all(nc, p, sems):
    with nc.Block() as block:
        def emit(eng_name, eng_obj):
            for fn, waits, inc, amt, drain in p.recs[eng_name]:
                for (s, v) in waits:
                    eng_obj.wait_ge(sems[s], v)
                inst = fn(eng_obj)
                if inc is not None:
                    if drain:
                        eng_obj.drain().then_inc(sems[inc], amt)
                    else:
                        assert inst is not None, (eng_name, inc)
                        inst.then_inc(sems[inc], amt)

        @block.sync
        def _(sync):
            emit("sync", sync)

        @block.tensor
        def _(tensor):
            emit("tensor", tensor)

        @block.vector
        def _(vector):
            emit("vector", vector)

        @block.scalar
        def _(scalar):
            emit("scalar", scalar)

        @block.gpsimd
        def _(gpsimd):
            emit("gpsimd", gpsimd)


_NC_CACHE = {}


def _get_nc():
    if "nc" not in _NC_CACHE:
        _NC_CACHE["nc"] = build_program()
    return _NC_CACHE["nc"]


def kernel(L0_attn_x, L0_attn_w_qkv, L0_attn_w_out, L0_ffn_w_router,
           L0_ffn_W_qkv, L0_ffn_W_out, L0_ffn_g1, L0_ffn_g2, L0_ffn_W1,
           L0_ffn_W2, L0_ffn_W3):
    x = np.ascontiguousarray(np.asarray(L0_attn_x, dtype=np.float32))
    consts = dict(
        c_identf=np.eye(128, dtype=np.float32),
        c_identb=np.eye(128, dtype=np.float32).astype(ml_dtypes.bfloat16),
        c_triu=(np.arange(128)[:, None] <= np.arange(128)[None, :]
                ).astype(ml_dtypes.bfloat16),
        c_tril=(np.arange(128)[:, None] < np.arange(128)[None, :]
                ).astype(ml_dtypes.bfloat16),
        c_iota512=np.tile(np.arange(512, dtype=np.float32)[None, :],
                          (128, 1)),
        c_idvec=(np.arange(128, dtype=np.float32)[:, None]
                 + 128.0 * np.arange(16, dtype=np.float32)[None, :]),
        c_onesb=np.ones((128, 1), np.float32).astype(ml_dtypes.bfloat16),
    )
    wr = np.asarray(L0_ffn_w_router, dtype=np.float32)
    weights = dict(
        aw_qkv=np.ascontiguousarray(np.asarray(L0_attn_w_qkv, np.float32)),
        aw_out=np.ascontiguousarray(np.asarray(L0_attn_w_out, np.float32)),
        fw_qkv=np.ascontiguousarray(np.asarray(L0_ffn_W_qkv, np.float32)),
        fw_out=np.ascontiguousarray(np.asarray(L0_ffn_W_out, np.float32)),
        w1=np.ascontiguousarray(np.asarray(L0_ffn_W1, np.float32)),
        w2=np.ascontiguousarray(np.asarray(L0_ffn_W2, np.float32)),
        w3=np.ascontiguousarray(np.asarray(L0_ffn_W3, np.float32)),
        wr_in=np.ascontiguousarray(np.tile(wr[None, :], (128, 1))),
        g1_in=np.ascontiguousarray(
            np.tile(np.asarray(L0_ffn_g1, np.float32)[None, :],
                    (128, 1))).astype(ml_dtypes.bfloat16),
        g2_in=np.ascontiguousarray(
            np.tile(np.asarray(L0_ffn_g2, np.float32)[None, :],
                    (128, 1))).astype(ml_dtypes.bfloat16),
    )

    nc = _get_nc()
    in_maps = []
    for core in range(8):
        m = dict(weights)
        m.update(consts)
        m["x_in"] = np.ascontiguousarray(x[2 * core:2 * core + 2])
        in_maps.append(m)
    res = run_bass_kernel_spmd(nc, in_maps, core_ids=list(range(8)))
    return np.concatenate([r["out"] for r in res.results], axis=0)
